# version 03
# baseline (speedup 1.0000x reference)
import contextlib
import math

import numpy as np
import ml_dtypes

import concourse.bass as bass
import concourse.mybir as mybir
from concourse.bass_utils import run_bass_kernel_spmd

F32 = mybir.dt.float32
BF16 = mybir.dt.bfloat16
AF = mybir.ActivationFunctionType
OP = mybir.AluOpType
AX = mybir.AxisListType

B, L, E, H, Dh = 8, 4096, 1024, 16, 64
NT = L // 128
NE = E // 128
SEQS = L // B

C52 = 6755399441055744.0
C32 = 12582912.0
SQ2 = math.sqrt(2.0)

CONSTS = [
    ("c52", C52), ("c32", C32), ("two", 2.0), ("epsln", 1e-5),
    ("one", 1.0), ("negone", -1.0), ("zero", 0.0), ("eps6", 1e-6),
    ("posk", math.pi / 2.0 / L), ("halfpi", math.pi / 2.0),
    ("n2048", 2048.0), ("u3bias", 511.5078125), ("neg64", -64.0),
    ("p32768", 32768.0), ("erfl", -8.0 / SQ2), ("erfr", -7.75 / SQ2),
    ("xlh", -4.0), ("xrh", -3.875), ("inv64", 1.0 / 64.0),
    ("n1024", 1024.0), ("n3072", 3072.0), ("neg8", -8.0),
    ("negbig", -1e30), ("half", 0.5), ("inv1024", 1.0 / 1024.0),
    ("neginv1024", -1.0 / 1024.0), ("inv32", 1.0 / 32.0),
]
CIDX = {n: i for i, (n, _) in enumerate(CONSTS)}

ENGINES = ("sync", "tensor", "vector", "scalar", "gpsimd")


class _Tok:
    __slots__ = ("sem", "eng", "dma", "pos", "inc", "val", "fd")


class Prog:

    def __init__(self):
        self.ops = {e: [] for e in ENGINES}
        self.dma_last = {}
        self.eng_last = {}

    def op(self, eng, fn, waits=(), dma=False, fd=1024):
        lst = self.ops[eng]
        need = 0
        ws = [w for w in waits if w is not None]
        for t in ws:
            if t.eng == eng and not t.dma and t.fd < 128:
                gap = len(lst) - t.pos
                if gap < 2:
                    need = max(need, 2 - gap)
        for _ in range(need):
            lst.append((None, (), None, "nop"))
        tok = _Tok()
        tok.eng, tok.dma, tok.inc, tok.fd = eng, dma, False, fd
        tok.pos = len(lst)
        tok.sem = ("d_" if dma else "c_") + eng
        if dma:
            tok.inc = True
        lst.append((fn, tuple(ws), tok, "op"))
        if dma:
            self.dma_last[eng] = tok
        else:
            self.eng_last[eng] = tok
        return tok

    def wait_only(self, eng, waits):
        ws = [w for w in waits if w is not None]
        self.ops[eng].append((None, tuple(ws), None, "wait"))

    def barrier(self):
        toks = list(self.eng_last.values()) + list(self.dma_last.values())
        for e in ENGINES:
            self.wait_only(e, [t for t in toks if not (t.eng == e and not t.dma)])

    def finalize_marks(self):
        for e in ENGINES:
            for fn, ws, tok, kind in self.ops[e]:
                for t in ws:
                    if t.eng != e or t.dma:
                        t.inc = True
        for e in ENGINES:
            cnt = {}
            for fn, ws, tok, kind in self.ops[e]:
                if tok is not None and tok.inc:
                    amt = 16 if tok.dma else 1
                    cnt[tok.sem] = cnt.get(tok.sem, 0) + amt
                    tok.val = cnt[tok.sem]

    def runner(self, eng_name, sems, nopfn):
        ops = self.ops[eng_name]

        def run(eng):
            sat = {}
            for fn, ws, tok, kind in ops:
                for t in ws:
                    if t.eng == eng_name and not t.dma:
                        continue
                    if sat.get(t.sem, 0) >= t.val:
                        continue
                    eng.wait_ge(sems[t.sem], t.val)
                    sat[t.sem] = t.val
                if kind == "nop":
                    nopfn(eng)
                    continue
                if kind == "wait":
                    continue
                inst = fn(eng)
                if tok.inc:
                    inst.then_inc(sems[tok.sem], 16 if tok.dma else 1)
        return run


class KB:

    def __init__(self, nc, nblocks=4, use_cc=True):
        self.nc = nc
        self.P = Prog()
        self.nblocks = nblocks
        self.use_cc = use_cc
        self.es = contextlib.ExitStack()
        self._uid = 0

    def uid(self, p):
        self._uid += 1
        return f"{p}{self._uid}"

    def sb(self, name, shape, dt=F32, stack=None):
        return (stack or self.es).enter_context(
            self.nc.sbuf_tensor(self.uid(name), shape, dt))

    def ps(self, name, shape, dt=F32):
        return self.es.enter_context(
            self.nc.psum_tensor(self.uid(name), shape, dt))

    def dma(self, q, out, in_, waits=(), slow=False):
        if slow:
            return self.P.op(
                q, lambda e, o=out, i=in_: e.dma_start(
                    out=o, in_=i, allow_slow_non_contiguous=True),
                waits=waits, dma=True)
        return self.P.op(q, lambda e, o=out, i=in_: e.dma_start(out=o, in_=i),
                         waits=waits, dma=True)

    def mm(self, out, lhsT, rhs, start, stop, waits=(), fd=512):
        return self.P.op("tensor",
                         lambda e, o=out, l=lhsT, r=rhs, s1=start, s2=stop:
                         e.matmul(o, l, r, start=s1, stop=s2),
                         waits=waits, fd=fd)

    def tp(self, out, in_, waits=()):
        idb = self.identb[:, :]
        return self.P.op("tensor",
                         lambda e, o=out, i=in_, d=idb: e.transpose(o, i, d),
                         waits=waits, fd=128)

    def act(self, out, in_, func, bias=0.0, scale=1.0, accum_out=None,
            waits=(), fd=None):
        out, in_, bias, accum_out = (_as_ap(out), _as_ap(in_), _as_ap(bias),
                                     _as_ap(accum_out))
        if fd is None:
            fd = _fd_of(out)
        return self.P.op("scalar",
                         lambda e, o=out, i=in_, f=func, b=bias, s=scale,
                         a=accum_out:
                         e.activation(out=o, in_=i, func=f, bias=b, scale=s,
                                      accum_out=a),
                         waits=waits, fd=fd)

    def vop(self, kind, *args, waits=(), fd=None, eng="vector"):
        args = [_as_ap(a) for a in args]
        if fd is None:
            fd = _fd_of(args[0])
        if kind == "ts":
            out, in0, s1, s2, op0, op1 = (list(args) + [None, None])[:6]
            if op1 is None:
                fn = lambda e, o=out, i=in0, a=s1, p0=op0: \
                    e.tensor_scalar(o, i, a, None, p0)
            else:
                fn = lambda e, o=out, i=in0, a=s1, b=s2, p0=op0, p1=op1: \
                    e.tensor_scalar(o, i, a, b, p0, p1)
        elif kind == "tt":
            out, in0, in1, op0 = args
            fn = lambda e, o=out, i=in0, j=in1, p=op0: e.tensor_tensor(o, i, j, p)
        elif kind == "stt":
            out, in0, s, in1, op0, op1 = args
            fn = lambda e, o=out, i=in0, a=s, j=in1, p0=op0, p1=op1: \
                e.scalar_tensor_tensor(o, i, a, j, p0, p1)
        elif kind == "copy":
            out, in0 = args
            fn = lambda e, o=out, i=in0: e.tensor_copy(o, i)
        elif kind == "red":
            out, in0, ax, op0, neg = args
            fn = lambda e, o=out, i=in0, a=ax, p=op0, n=neg: \
                e.tensor_reduce(o, i, a, p, negate=n)
        elif kind == "recip":
            out, in0 = args
            fn = lambda e, o=out, i=in0: e.reciprocal(o, i)
        elif kind == "memset":
            out, val = args
            fn = lambda e, o=out, v=val: e.memset(o, v)
        else:
            raise ValueError(kind)
        return self.P.op(eng, fn, waits=waits, fd=fd)

    def iota(self, out, pattern, base, mult, waits=(), fd=32):
        return self.P.op("gpsimd",
                         lambda e, o=out, p=pattern, b=base, m=mult:
                         e.iota(o, pattern=p, base=b, channel_multiplier=m,
                                allow_small_or_imprecise_dtypes=True),
                         waits=waits, fd=fd)

    def cb(self, name):
        return self.cbs[:, CIDX[name]:CIDX[name] + 1]

    def split(self, hi, lo, x, waits=(), fd=None):
        t1 = self.vop("copy", hi, x, waits=waits, fd=fd)
        t2 = self.vop("stt", lo, x, 1.0, hi, OP.mult, OP.subtract,
                      waits=[t1], fd=fd)
        return t2

    def ln(self, out, x, sc, waits=()):
        t1 = self.act(sc["dump"], x, AF.Copy, accum_out=sc["rsum"],
                      waits=waits, fd=1024)
        t2 = self.vop("ts", sc["negmean"], sc["rsum"], -1.0 / 1024.0, None,
                      OP.mult, waits=[t1], fd=1)
        t3 = self.act(sc["dump"], x, AF.Square, bias=sc["negmean"],
                      accum_out=sc["sqs"], waits=[t2], fd=1024)
        t3b = self.vop("copy", sc["sqsb"], sc["sqs"], waits=[t3], fd=1,
                       eng="gpsimd")
        t4 = self.act(sc["std"], sc["sqsb"], AF.Sqrt, bias=self.cb("epsln"),
                      scale=1.0 / 1024.0, waits=[t3b], fd=1)
        t5 = self.vop("recip", sc["rstd"], sc["std"], waits=[t4], fd=1)
        t5b = self.vop("copy", sc["rstdb"], sc["rstd"], waits=[t5], fd=1,
                       eng="gpsimd")
        t6 = self.vop("ts", sc["cent"], x, sc["negmean"], None, OP.add,
                      waits=[t2], fd=1024)
        t7 = self.vop("ts", out, sc["cent"], sc["rstdb"], None, OP.mult,
                      waits=[t5b, t6], fd=1024)
        return t7

    def rne(self, fl, u, sc, waits=(), fd=None):
        ta = self.vop("ts", sc["a"], u, self.cb("c52"), None, OP.add,
                      waits=waits, fd=fd)
        tf1 = self.vop("ts", sc["f1"], sc["a"], self.negc52, None, OP.add,
                       waits=[ta], fd=fd)
        tb = self.vop("tt", sc["b"], u, sc["f1"], OP.subtract,
                      waits=[tf1], fd=fd)
        tc = self.vop("ts", sc["c"], sc["b"], self.cb("c32"), None, OP.add,
                      waits=[tb], fd=fd)
        tf2 = self.vop("ts", sc["f2"], sc["c"], self.negc32, None, OP.add,
                       waits=[tc], fd=fd)
        tgt = self.vop("ts", sc["gt"], sc["f2"], self.cb("two"), None,
                       OP.is_ge, waits=[tf2], fd=fd)
        tg = self.vop("tt", sc["g"], sc["gt"], sc["f2"], OP.mult,
                      waits=[tgt], fd=fd)
        tfl = self.vop("tt", fl, sc["f1"], sc["g"], OP.add,
                       waits=[tg, tf1], fd=fd)
        return tfl


def _as_ap(x):
    if x is None or isinstance(x, (int, float, bass.AP)):
        return x
    if hasattr(x, "ap"):
        return x.ap()
    return x


def _fd_of(ap):
    try:
        sh = ap.shape
        n = 1
        for s in sh[1:]:
            n *= int(s)
        return n
    except Exception:
        return 1024


def _np_split(x):
    hi = x.astype(ml_dtypes.bfloat16)
    return hi


def build_nc(nblocks=4, use_cc=True, dbg=()):
    nc = bass.Bass("TRN2", target_bir_lowering=False)
    kb = KB(nc, nblocks, use_cc)
    kb._dbg = dbg
    P = kb.P

    def din(name, shape, dt=F32):
        return nc.dram_tensor(name, shape, dt, kind="ExternalInput").ap()

    xb = din("xb", [L, E])
    xs = din("xs", [B, SEQS, E])
    posc = din("posc", [SEQS])
    cbd = din("cbt", [128, len(CONSTS)])
    identd = din("identb", [128, 128], BF16)
    iota16d = din("iota16", [128, 16])

    wnames = {}
    for nm, shape in [
        ("L0Wq", [E, E]), ("L0Wk", [E, E]), ("L0Wv", [E, E]), ("L0Wo", [E, E]),
        ("L0bq", [E]), ("L0bk", [E]), ("L0bv", [E]), ("L0bo", [E]),
        ("nodecat", [E, 48]), ("nodeb", [48]),
        ("W1q", [E, E]), ("W1k", [E, E]), ("W1v", [E, E]),
        ("W2q", [E, E]), ("W2k", [E, E]), ("W2v", [E, E]),
        ("b1q", [E]), ("b1k", [E]), ("b1v", [E]),
        ("b2q", [16, E]), ("b2k", [16, E]), ("b2v", [16, E]),
        ("L1WqT", [E, E]), ("L1WkT", [E, E]), ("L1WvT", [E, E]),
        ("L1WoT", [E, E]),
        ("F1w1", [E, 4 * E]), ("F1b1", [4 * E]), ("F1w2", [4 * E, E]),
        ("F1b2", [E]),
    ]:
        wnames[nm] = din(nm, shape)

    out_d = nc.dram_tensor("out", [L, E], F32, kind="ExternalOutput").ap()

    def dint(name, shape, dt=F32):
        kind = "ExternalOutput" if name in dbg else "Internal"
        return nc.dram_tensor(name, shape, dt, kind=kind).ap()

    attnln_d = dint("attnln", [L, E])
    stream_d = dint("stream", [L, E])
    stream2_d = dint("stream2", [L, E])
    lnT_d = [dint("lnTh", [NE, 128, L], BF16), dint("lnTl", [NE, 128, L], BF16)]
    qT_d = [dint("qTh", [NE, 128, L], BF16), dint("qTl", [NE, 128, L], BF16)]
    kT_d = [dint("kTh", [NE, 128, L], BF16), dint("kTl", [NE, 128, L], BF16)]
    v_d = [dint("vh", [L, E], BF16), dint("vl", [L, E], BF16)]
    yqT_d = [dint("yqTh", [32, 128, L], BF16), dint("yqTl", [32, 128, L], BF16)]
    outh0_d = dint("outh0", [L, 512])
    kvin_d = dint("kvin", [128, 2080])
    kvall_d = dint("kvall", [128 * B, 2080]) if use_cc else None

    es = kb.es
    with es:
        kb.cbs = kb.sb("cbs", [128, len(CONSTS)])
        kb.nopbuf = kb.sb("nopb", [128, 1])
        kb.identb = kb.sb("idb", [128, 128], BF16)
        iota16 = kb.sb("io16", [128, 16])
        ones1 = kb.sb("ones1", [1, 128], BF16)
        negc52 = kb.sb("nc52", [128, 1])
        negc32 = kb.sb("nc32", [128, 1])
        kb.negc52 = negc52
        kb.negc32 = negc32
        wstage = kb.sb("wstage", [128, 4096])
        kb._wstage_tok = None

        sc = {n: kb.sb(n, [128, 1]) for n in
              ("rsum", "negmean", "sqs", "std", "rstd", "sqsb", "rstdb")}
        sc["dump"] = kb.sb("dump", [128, 1024])
        sc["cent"] = kb.sb("cent", [128, 1024])

        pb = [kb.ps(f"pb{i}", [128, 512], F32) for i in range(6)]
        ptA = kb.ps("ptA", [128, 1024], BF16)
        ptB = kb.ps("ptB", [128, 1024], BF16)

        t_cb = kb.dma("sync", kb.cbs[:, :], cbd[:, :])
        t_id = kb.dma("sync", kb.identb[:, :], identd[:, :])
        t_i16 = kb.dma("sync", iota16[:, :], iota16d[:, :])
        t_o1 = kb.vop("memset", ones1[:, :], 1.0, fd=128)
        t_n52 = kb.vop("ts", negc52, kb.cb("c52"), -1.0, None, OP.mult,
                       waits=[t_cb], fd=1)
        t_n32 = kb.vop("ts", negc32, kb.cb("c32"), -1.0, None, OP.mult,
                       waits=[t_cb], fd=1)
        P.barrier()

        def load_w(dram_ap, K, N, stack):
            kc = K // 128
            hi = kb.sb("wh", [128, kc, N], BF16, stack)
            lo = kb.sb("wl", [128, kc, N], BF16, stack)
            src = dram_ap.rearrange("(c p) n -> p c n", p=128)
            step = max(1, 4096 // N)
            ts_ = None
            for c0 in range(0, kc, step):
                c1 = min(kc, c0 + step)
                st = wstage[:, 0:(c1 - c0) * N].rearrange(
                    "p (c n) -> p c n", n=N)
                td = kb.dma("sync", st, src[:, c0:c1, :],
                            waits=[kb._wstage_tok])
                ts_ = kb.split(hi[:, c0:c1, :], lo[:, c0:c1, :], st,
                               waits=[td], fd=(c1 - c0) * N)
                kb._wstage_tok = ts_
            return hi, lo, ts_

        def load_bias(dram_ap, N, stack):
            hi = kb.sb("bh", [1, N], BF16, stack)
            lo = kb.sb("bl", [1, N], BF16, stack)
            src2 = dram_ap.rearrange("(o n) -> o n", o=1)
            ts_ = None
            for c0 in range(0, N, 2048):
                c1 = min(N, c0 + 2048)
                st = wstage[0:1, 0:c1 - c0]
                td = kb.dma("sync", st, src2[:, c0:c1],
                            waits=[kb._wstage_tok])
                ts_ = kb.split(hi[:, c0:c1], lo[:, c0:c1], st, waits=[td],
                               fd=c1 - c0)
                kb._wstage_tok = ts_
            return hi, lo, ts_

        def dense(psums, lnT, w, bias, waits, nhalf=2, ncol=512):
            toks = []
            for h in range(nhalf):
                cs, ce = h * ncol, (h + 1) * ncol
                n_mm = NE * 3 + (2 if bias is not None else 0)
                i = 0
                tok = None
                for e in range(NE):
                    for (a, b_) in ((0, 0), (0, 1), (1, 0)):
                        tok = kb.mm(psums[h][:, 0:ncol], lnT[a][:, e, :],
                                    w[b_][:, e, cs:ce],
                                    start=(i == 0), stop=(i == n_mm - 1),
                                    waits=(waits if i == 0 else ()), fd=ncol)
                        i += 1
                if bias is not None:
                    for b_ in range(2):
                        tok = kb.mm(psums[h][:, 0:ncol], ones1[:, :],
                                    bias[b_][:, cs:ce],
                                    start=False, stop=(i == n_mm - 1), fd=ncol)
                        i += 1
                toks.append(tok)
            return toks

        def prep_lnT(x_tile, lnf, lnhi, lnlo, lnThi, lnTlo, waits,
                     pt=(None, None), guard2=()):
            ptA_, ptB_ = pt
            t_ln = kb.ln(lnf, x_tile, sc, waits=waits)
            t_sp = kb.split(lnhi, lnlo, lnf,
                            waits=[t_ln, getattr(kb, "_prep_tps", None)],
                            fd=1024)
            tps = []
            for e in range(NE):
                tps.append(kb.tp(ptA_[:, e * 128:(e + 1) * 128],
                                 lnhi[:, e * 128:(e + 1) * 128],
                                 waits=([t_sp] if e == 0 else ())))
            for e in range(NE):
                tps.append(kb.tp(ptB_[:, e * 128:(e + 1) * 128],
                                 lnlo[:, e * 128:(e + 1) * 128],
                                 waits=([t_sp] if e == 0 else ())))
            kb._prep_tps = tps[-1]
            tv1 = kb.act(lnThi[:, :, :], ptA_[:, :].rearrange(
                "p (c n) -> p c n", n=128), AF.Copy,
                waits=[tps[NE - 1]] + list(guard2), fd=1024)
            tv2 = kb.act(lnTlo[:, :, :], ptB_[:, :].rearrange(
                "p (c n) -> p c n", n=128), AF.Copy,
                waits=[tps[-1]] + list(guard2), fd=1024)
            return t_ln, [tv1, tv2]

        with contextlib.ExitStack() as ph:
            wk = load_w(wnames["L0Wk"], E, E, ph)
            wv = load_w(wnames["L0Wv"], E, E, ph)
            bk = load_bias(wnames["L0bk"], E, ph)
            bv = load_bias(wnames["L0bv"], E, ph)
            P.barrier()
            if "wkdbg" in dbg:
                wkh_d = nc.dram_tensor("wkhdbg", [128, 8 * 1024], BF16,
                                       kind="ExternalOutput").ap()
                wkl_d = nc.dram_tensor("wkldbg", [128, 8 * 1024], BF16,
                                       kind="ExternalOutput").ap()
                kb.dma("sync", wkh_d[:, :],
                       wk[0][:, :, :].rearrange("p c n -> p (c n)"))
                kb.dma("sync", wkl_d[:, :],
                       wk[1][:, :, :].rearrange("p c n -> p (c n)"))
                P.barrier()

            ksb = kb.sb("ksb", [128, 1024], F32, ph)
            xt = [kb.sb("xA", [128, 1024], F32, ph) for _ in range(2)]
            lnf = kb.sb("lnA", [128, 1024], F32, ph)
            lnhi = kb.sb("lnAh", [128, 1024], BF16, ph)
            lnlo = kb.sb("lnAl", [128, 1024], BF16, ph)
            lnThi = kb.sb("lnATh", [128, NE, 128], BF16, ph)
            lnTlo = kb.sb("lnATl", [128, NE, 128], BF16, ph)
            vsb = kb.sb("vsb", [128, 1024], F32, ph)
            nmax = kb.sb("nmax", [128, 16], F32, ph)
            nmaxb = kb.sb("nmaxb", [128, 16], F32, ph)
            ssumb = kb.sb("ssumb", [128, 16], F32, ph)
            ssum = kb.sb("ssum", [128, 16], F32, ph)
            srec = kb.sb("srec", [128, 16], F32, ph)
            srecb = kb.sb("srecb", [128, 16], F32, ph)
            kn = kb.sb("kn", [128, 1024], F32, ph)
            ckn = kb.sb("ckn", [128, 1024], F32, ph)
            kff = kb.sb("kff", [128, 16, 128], F32, ph)
            kfh = kb.sb("kfh", [128, 2, 16, 128], BF16, ph)
            kfl = kb.sb("kfl", [128, 2, 16, 128], BF16, ph)
            vah = kb.sb("vah", [128, 16, 65], BF16, ph)
            val_ = kb.sb("val", [128, 16, 65], BF16, ph)
            kvacc = kb.sb("kvacc", [128, 2080], F32, ph)
            posA = kb.sb("posA", [128, NT], F32, ph)
            sinA = kb.sb("sinA", [128, NT], F32, ph)
            cosA = kb.sb("cosA", [128, NT], F32, ph)
            idxA = kb.sb("idxA", [128, NT], F32, ph)

            t_io = kb.iota(posA[:, :], [[128, NT]], 1, 1, fd=NT)
            t_ix = kb.act(idxA[:, :], posA[:, :], AF.Copy,
                          scale=math.pi / 2.0 / L, waits=[t_io], fd=NT)
            t_sn = kb.act(sinA[:, :], idxA[:, :], AF.Sin, waits=[t_ix], fd=NT)
            t_cs = kb.act(cosA[:, :], idxA[:, :], AF.Sin, scale=-1.0,
                          bias=kb.cb("halfpi"), waits=[t_ix], fd=NT)
            t_va1 = kb.vop("memset", vah[:, :, 64:65], 1.0, fd=16)
            t_va0 = kb.vop("memset", val_[:, :, 64:65], 0.0, fd=16)

            tok_x = [None, None]
            tok_acc = None
            prev_use = [None, None]
            mm_kv_prev = [None, None]
            prev_kfdump = None
            prev_tkn = None
            pd_ksb = pd_kex = pd_kn = None
            if "pipeA" in dbg:
                ksball_d = nc.dram_tensor("ksball", [NT, 128, 1024], F32,
                                          kind="ExternalOutput").ap()
                kexall_d = nc.dram_tensor("kexall", [NT, 128, 1024], F32,
                                          kind="ExternalOutput").ap()
                knall_d = nc.dram_tensor("knall", [NT, 128, 1024], F32,
                                         kind="ExternalOutput").ap()
            if "kfall" in dbg:
                kfall_d = nc.dram_tensor("kfall", [NT, 128, 4096], BF16,
                                         kind="ExternalOutput").ap()
            for t in range(NT):
                par = t % 2
                tok_x[par] = kb.dma("sync", xt[par][:, :],
                                    xb[t * 128:(t + 1) * 128, :],
                                    waits=[prev_use[par]])
                t_ln, t_lnT = prep_lnT(xt[par][:, :], lnf, lnhi, lnlo,
                                       lnThi, lnTlo,
                                       waits=[tok_x[par]], pt=(ptA, ptB))
                prev_use[par] = t_ln
                tk = dense([pb[0], pb[1]], (lnThi, lnTlo), wk[:2], bk[:2],
                           waits=t_lnT)
                tv = dense([pb[2], pb[3]], (lnThi, lnTlo), wv[:2], bv[:2],
                           waits=t_lnT)
                te1 = kb.act(ksb[:, 0:512], pb[0][:, :], AF.Copy,
                             waits=[tk[0], pd_ksb, prev_tkn], fd=512)
                te2 = kb.act(ksb[:, 512:1024], pb[1][:, :], AF.Copy,
                             waits=[tk[1]], fd=512)
                te3 = kb.act(vsb[:, 0:512], pb[2][:, :], AF.Copy,
                             waits=[tv[0]], fd=512)
                te4 = kb.act(vsb[:, 512:1024], pb[3][:, :], AF.Copy,
                             waits=[tv[1]], fd=512)
                k3 = ksb[:, :].rearrange("p (h d) -> p h d", d=64)
                t_m = kb.vop("red", nmax[:, :], k3, AX.X, OP.max, True,
                             waits=[te1, te2], fd=16)
                t_mb = kb.vop("copy", nmaxb[:, :], nmax[:, :], waits=[t_m],
                              fd=16, eng="gpsimd")
                ks2v = kff[:, :, 0:64]
                t_s = kb.vop("tt", ks2v, k3,
                             nmaxb[:, :].rearrange("p (h o) -> p h o", o=1)
                             .broadcast_to([128, 16, 64]), OP.add,
                             waits=[t_mb], fd=1024)
                kex = ksb
                t_e = kb.act(kex[:, :].rearrange("p (h d) -> p h d", d=64),
                             ks2v, AF.Exp, waits=[t_s, pd_kex], fd=1024)
                t_r = kb.vop("red", ssum[:, :],
                             kex[:, :].rearrange("p (h d) -> p h d", d=64),
                             AX.X, OP.add, False, waits=[t_e], fd=16)
                t_sb = kb.vop("copy", ssumb[:, :], ssum[:, :], waits=[t_r],
                              fd=16, eng="gpsimd")
                t_rc = kb.vop("recip", srec[:, :], ssumb[:, :],
                              waits=[t_sb], fd=16)
                t_rb = kb.vop("copy", srecb[:, :], srec[:, :], waits=[t_rc],
                              fd=16, eng="gpsimd")
                t_kn = kb.vop("tt", kn[:, :].rearrange("p (h d) -> p h d", d=64),
                              kex[:, :].rearrange("p (h d) -> p h d", d=64),
                              srecb[:, :].rearrange("p (h o) -> p h o", o=1)
                              .broadcast_to([128, 16, 64]), OP.mult,
                              waits=[t_rb, pd_kn], fd=1024)
                prev_tkn = t_kn
                if "pipeA" in dbg:
                    pd_ksb = kb.dma("sync", ksball_d[t, :, :], ksb[:, :],
                                    waits=[te1, te2])
                    pd_kex = kb.dma("sync", kexall_d[t, :, :], kex[:, :],
                                    waits=[t_e])
                    pd_kn = kb.dma("sync", knall_d[t, :, :], kn[:, :],
                                   waits=[t_kn])
                t_ck = kb.vop("ts", ckn[:, :], kn[:, :], -1.0, 1.0,
                              OP.mult, OP.add, waits=[t_kn], fd=1024)
                kn3 = kn[:, :].rearrange("p (h d) -> p h d", d=64)
                ck3 = ckn[:, :].rearrange("p (h d) -> p h d", d=64)
                t_kf = None
                for br, base, t_b in ((0, kn3, t_kn), (1, ck3, t_ck)):
                    tf1 = kb.vop("ts", kff[:, :, 0:64], base,
                                 sinA[:, t:t + 1], None, OP.mult,
                                 waits=[t_b, t_sn, t_kf], fd=1024)
                    tf2 = kb.vop("ts", kff[:, :, 64:128], base,
                                 cosA[:, t:t + 1], None, OP.mult,
                                 waits=[t_b, t_cs, t_kf], fd=1024)
                    t_kf = kb.split(kfh[:, br, :, :], kfl[:, br, :, :],
                                    kff[:, :, :],
                                    waits=[tf1, tf2, (prev_kfdump
                                                      if br == 0 else None)],
                                    fd=2048)
                v3 = vsb[:, :].rearrange("p (h d) -> p h d", d=64)
                tv1 = kb.vop("copy", vah[:, :, 0:64], v3, waits=[te3, te4],
                             fd=1024)
                tv2 = kb.vop("stt", val_[:, :, 0:64], v3, 1.0,
                             vah[:, :, 0:64], OP.mult, OP.subtract,
                             waits=[tv1], fd=1024)
                if "kfall" in dbg:
                    prev_kfdump = kb.dma(
                        "sync", kfall_d[t, :, :],
                        kfh[:, :, :, :].rearrange("p a b c -> p (a b c)"),
                        waits=[t_kf])
                if t == 0 and "smdbg" in dbg:
                    smdbg_d = nc.dram_tensor("smdbg", [128, 32], F32,
                                             kind="ExternalOutput").ap()
                    kndbg_d = nc.dram_tensor("kndbg", [128, 2048], F32,
                                             kind="ExternalOutput").ap()
                    kb.dma("sync", smdbg_d[:, 0:16], nmax[:, :], waits=[t_kn])
                    kb.dma("sync", smdbg_d[:, 16:32], srec[:, :],
                           waits=[t_kn])
                    kb.dma("sync", kndbg_d[:, 0:1024], kn[:, :], waits=[t_kn])
                    kb.dma("sync", kndbg_d[:, 1024:2048], kex[:, :],
                           waits=[t_kn])
                if t == 0 and "kfdbgA" in dbg:
                    kfdbgA_d = nc.dram_tensor("kfdbgA", [128, 2 * 16 * 128],
                                              BF16, kind="ExternalOutput").ap()
                    vadbgA_d = nc.dram_tensor("vadbgA", [128, 16 * 65],
                                              BF16, kind="ExternalOutput").ap()
                    kb.dma("sync", kfdbgA_d[:, :],
                           kfh[:, :, :, :].rearrange("p a b c -> p (a b c)"),
                           waits=[t_kf])
                    kb.dma("sync", vadbgA_d[:, :],
                           vah[:, :, :].rearrange("p a b -> p (a b)"),
                           waits=[tv2, t_va1])
                kv3 = kvacc[:, :].rearrange("p (h b n) -> p h b n", b=2, n=65)
                for h in range(H - 1, -1, -1):
                    for br in range(2):
                        bank = pb[4 + (h * 2 + br) % 2]
                        lw = [t_kf, tv2, t_va1, t_va0,
                              mm_kv_prev[(h * 2 + br) % 2]]
                        m1 = kb.mm(bank[:, 0:65], kfh[:, br, h, :],
                                   vah[:, h, :], True, False, waits=lw, fd=65)
                        kb.mm(bank[:, 0:65], kfh[:, br, h, :],
                              val_[:, h, :], False, False, fd=65)
                        m3 = kb.mm(bank[:, 0:65], kfl[:, br, h, :],
                                   vah[:, h, :], False, True, fd=65)
                        if t == 0:
                            ta = kb.vop("copy", kv3[:, h, br, :],
                                        bank[:, 0:65], waits=[m3], fd=65)
                        else:
                            ta = kb.vop("tt", kv3[:, h, br, :],
                                        kv3[:, h, br, :], bank[:, 0:65],
                                        OP.add, waits=[m3, tok_acc], fd=65)
                        mm_kv_prev[(h * 2 + br) % 2] = ta
                        tok_acc = ta
                if t == 0 and "kvdbg0" in dbg:
                    kvdbg0_d = nc.dram_tensor("kvdbg0", [128, 2080], F32,
                                              kind="ExternalOutput").ap()
                    kb.dma("sync", kvdbg0_d[:, :], kvacc[:, :],
                           waits=[tok_acc])
            if "wkdbg2" in dbg:
                wkh2_d = nc.dram_tensor("wkhdbg2", [128, 8 * 1024], BF16,
                                        kind="ExternalOutput").ap()
                kb.dma("sync", wkh2_d[:, :],
                       wk[0][:, :, :].rearrange("p c n -> p (c n)"),
                       waits=[tok_acc])
            t_kvout = kb.dma("gpsimd", kvin_d[:, :], kvacc[:, :],
                             waits=[tok_acc])
            if "kvdbg" in dbg:
                kvdbg_d = nc.dram_tensor("kvdbg", [128, 2080], F32,
                                         kind="ExternalOutput").ap()
                kb.dma("sync", kvdbg_d[:, :], kvacc[:, :], waits=[tok_acc])
            P.barrier()

        if use_cc:
            t_cc = P.op("gpsimd",
                        lambda e: e.collective_compute(
                            "AllGather", OP.bypass,
                            ins=[kvin_d[:, :]], outs=[kvall_d[:, :]],
                            replica_groups=[[0, 1, 2, 3, 4, 5, 6, 7]]),
                        waits=[t_kvout], fd=2080)
        P.barrier()

        with contextlib.ExitStack() as ph:
            wq = load_w(wnames["L0Wq"], E, E, ph)
            wo = load_w(wnames["L0Wo"], E, E, ph)
            bq = load_bias(wnames["L0bq"], E, ph)
            bo = load_bias(wnames["L0bo"], E, ph)
            P.barrier()

            xt = [kb.sb("xC", [128, 1024], F32, ph) for _ in range(2)]
            lnf = kb.sb("lnC", [128, 1024], F32, ph)
            lnhi = kb.sb("lnCh", [128, 1024], BF16, ph)
            lnlo = kb.sb("lnCl", [128, 1024], BF16, ph)
            lnThi = kb.sb("lnCTh", [128, NE, 128], BF16, ph)
            lnTlo = kb.sb("lnCTl", [128, NE, 128], BF16, ph)
            qsb = kb.sb("qsb", [128, 1024], F32, ph)
            nmax = kb.sb("nmaxC", [128, 16], F32, ph)
            nmaxb = kb.sb("nmaxbC", [128, 16], F32, ph)
            ssumb = kb.sb("ssumbC", [128, 16], F32, ph)
            ssum = kb.sb("ssumC", [128, 16], F32, ph)
            srec = kb.sb("srecC", [128, 16], F32, ph)
            srecb = kb.sb("srecbC", [128, 16], F32, ph)
            qn = lnf
            qff = kb.sb("qff", [128, 16, 128], F32, ph)
            qfh = kb.sb("qfh", [128, 2, 16, 128], BF16, ph)
            qfl = kb.sb("qfl", [128, 2, 16, 128], BF16, ph)
            qTh = kb.sb("qTh", [128, 32, 128], BF16, ph)
            qTl = kb.sb("qTl", [128, 32, 128], BF16, ph)
            kvf = kb.sb("kvf", [128, 2080], F32, ph)
            kvh = kb.sb("kvh", [128, 16, 2, 65], BF16, ph)
            kvl = kb.sb("kvl", [128, 16, 2, 65], BF16, ph)
            asb = kb.sb("asb", [128, 16, 130], F32, ph)
            z0 = kb.sb("z0", [128, 16], F32, ph)
            z1 = kb.sb("z1", [128, 16], F32, ph)
            z0r = kb.sb("z0r", [128, 16], F32, ph)
            z1r = kb.sb("z1r", [128, 16], F32, ph)
            z0rb = kb.sb("z0rb", [128, 16], F32, ph)
            z1rb = kb.sb("z1rb", [128, 16], F32, ph)
            at0 = kb.sb("at0", [128, 1024], F32, ph)
            attn = kb.sb("attnC", [128, 1024], F32, ph)
            athi = kb.sb("athi", [128, 1024], BF16, ph)
            atlo = kb.sb("atlo", [128, 1024], BF16, ph)
            atThi = kb.sb("atThi", [128, NE, 128], BF16, ph)
            atTlo = kb.sb("atTlo", [128, NE, 128], BF16, ph)
            proj = kb.sb("projC", [128, 1024], F32, ph)
            pos4 = kb.sb("pos4", [128, 4], F32, ph)
            idx4 = kb.sb("idx4", [128, 4], F32, ph)
            sin4 = kb.sb("sin4", [128, 4], F32, ph)
            cos4 = kb.sb("cos4", [128, 4], F32, ph)
            nsin4 = kb.sb("nsin4", [128, 4], F32, ph)
            ncos4 = kb.sb("ncos4", [128, 4], F32, ph)

            t_p = kb.dma("sync", pos4[:, :],
                         posc.rearrange("(j p) -> p j", p=128), slow=True)
            t_ix = kb.vop("ts", idx4[:, :], pos4[:, :], math.pi / 2.0 / L,
                          kb.cb("posk"), OP.mult, OP.add, waits=[t_p], fd=4)
            t_sn = kb.act(sin4[:, :], idx4[:, :], AF.Sin, waits=[t_ix], fd=4)
            t_cs = kb.act(cos4[:, :], idx4[:, :], AF.Sin, scale=-1.0,
                          bias=kb.cb("halfpi"), waits=[t_ix], fd=4)
            t_nsn = kb.act(nsin4[:, :], sin4[:, :], AF.Copy, scale=-1.0,
                           waits=[t_sn], fd=4)
            t_ncs = kb.act(ncos4[:, :], cos4[:, :], AF.Copy, scale=-1.0,
                           waits=[t_cs], fd=4)

            attnln_r = attnln_d.rearrange("(i e) m -> e i m", e=8)
            tok_x = [None, None]
            prev_use = [None, None]
            t_kvb = None
            prev_od = None
            prev_tqn = None
            mm_prev = [None, None]
            for bp in range(B):
                src = kvall_d[128 * bp:128 * (bp + 1), :] if use_cc \
                    else kvin_d[:, :]
                t_kv = kb.dma("sync", kvf[:, :], src,
                              waits=[t_kvb, (t_cc if use_cc else t_kvout),
                                     mm_prev[0], mm_prev[1]])
                kv4 = kvf[:, :].rearrange("p (h b n) -> p h b n", b=2, n=65)
                t_kvb = kb.split(kvh[:, :, :, :], kvl[:, :, :, :], kv4,
                                 waits=[t_kv], fd=2080)
                for j in range(4):
                    par = (bp * 4 + j) % 2
                    tok_x[par] = kb.dma("sync", xt[par][:, :],
                                        xs[bp, j * 128:(j + 1) * 128, :],
                                        waits=[prev_use[par]])
                    t_ln, t_lnT = prep_lnT(xt[par][:, :], lnf, lnhi, lnlo,
                                           lnThi, lnTlo, waits=[tok_x[par]],
                                           pt=(ptA, ptB))
                    prev_use[par] = t_ln
                    tq = dense([pb[0], pb[1]], (lnThi, lnTlo), wq[:2],
                               bq[:2], waits=t_lnT)
                    te1 = kb.act(qsb[:, 0:512], pb[0][:, :], AF.Copy,
                                 waits=[tq[0], prev_tqn], fd=512)
                    te2 = kb.act(qsb[:, 512:1024], pb[1][:, :], AF.Copy,
                                 waits=[tq[1]], fd=512)
                    q3 = qsb[:, :].rearrange("p (h d) -> p h d", d=64)
                    t_m = kb.vop("red", nmax[:, :], q3, AX.X, OP.max, True,
                                 waits=[te1, te2], fd=16)
                    t_mb = kb.vop("copy", nmaxb[:, :], nmax[:, :],
                                  waits=[t_m], fd=16, eng="gpsimd")
                    qs2v = qff[:, :, 0:64]
                    t_s = kb.vop("tt", qs2v, q3,
                                 nmaxb[:, :].rearrange("p (h o) -> p h o", o=1)
                                 .broadcast_to([128, 16, 64]), OP.add,
                                 waits=[t_mb], fd=1024)
                    qex = qsb
                    t_e = kb.act(qex[:, :].rearrange("p (h d) -> p h d", d=64),
                                 qs2v, AF.Exp, waits=[t_s], fd=1024)
                    t_r = kb.vop("red", ssum[:, :],
                                 qex[:, :].rearrange("p (h d) -> p h d", d=64),
                                 AX.X, OP.add, False, waits=[t_e], fd=16)
                    t_sb = kb.vop("copy", ssumb[:, :], ssum[:, :],
                                  waits=[t_r], fd=16, eng="gpsimd")
                    t_rc = kb.vop("recip", srec[:, :], ssumb[:, :],
                                  waits=[t_sb], fd=16)
                    t_rb = kb.vop("copy", srecb[:, :], srec[:, :],
                                  waits=[t_rc], fd=16, eng="gpsimd")
                    t_qn = kb.vop("tt",
                                  qn[:, :].rearrange("p (h d) -> p h d", d=64),
                                  qex[:, :].rearrange("p (h d) -> p h d", d=64),
                                  srecb[:, :].rearrange("p (h o) -> p h o", o=1)
                                  .broadcast_to([128, 16, 64]), OP.mult,
                                  waits=[t_rb], fd=1024)
                    prev_tqn = t_qn
                    qn3 = qn[:, :].rearrange("p (h d) -> p h d", d=64)
                    tf1 = kb.vop("ts", qff[:, :, 0:64], qn3,
                                 sin4[:, j:j + 1], None, OP.mult,
                                 waits=[t_qn, t_sn], fd=1024)
                    tf2 = kb.vop("ts", qff[:, :, 64:128], qn3,
                                 cos4[:, j:j + 1], None, OP.mult,
                                 waits=[t_qn, t_cs], fd=1024)
                    t_qf = kb.split(qfh[:, 0, :, :], qfl[:, 0, :, :],
                                    qff[:, :, :], waits=[tf1, tf2], fd=2048)
                    tg1 = kb.vop("ts", qff[:, :, 0:64], qn3,
                                 nsin4[:, j:j + 1], sin4[:, j:j + 1],
                                 OP.mult, OP.add,
                                 waits=[t_nsn, t_qf], fd=1024)
                    tg2 = kb.vop("ts", qff[:, :, 64:128], qn3,
                                 ncos4[:, j:j + 1], cos4[:, j:j + 1],
                                 OP.mult, OP.add,
                                 waits=[t_ncs, t_qf], fd=1024)
                    t_qf = kb.split(qfh[:, 1, :, :], qfl[:, 1, :, :],
                                    qff[:, :, :], waits=[tg1, tg2], fd=2048)
                    qf2h = qfh[:, :, :, :].rearrange("p b h n -> p (b h) n")
                    qf2l = qfl[:, :, :, :].rearrange("p b h n -> p (b h) n")
                    tevs = []
                    for half, (srcq, dstq, bankq) in enumerate(
                            (((qf2h, qTh, ptA)), (qf2l, qTl, ptB))):
                        for rnd in range(4):
                            tpr = []
                            for i in range(8):
                                ci = rnd * 8 + i
                                tpr.append(kb.tp(
                                    bankq[:, i * 128:(i + 1) * 128],
                                    srcq[:, ci, :],
                                    waits=([t_qf, tevs[-1] if tevs else None]
                                           if i == 0 else ())))
                            tevs.append(kb.act(
                                dstq[:, rnd * 8:(rnd + 1) * 8, :],
                                bankq[:, :].rearrange("p (c n) -> p c n",
                                                      n=128),
                                AF.Copy, waits=[tpr[-1]], fd=1024))
                    for h in range(H):
                        bank = pb[2 + h % 2]
                        toks_mm = []
                        for br in range(2):
                            lw = [tevs[-1], t_kvb, mm_prev[h % 2]] \
                                if br == 0 else []
                            m1 = kb.mm(bank[:, br * 65:br * 65 + 65],
                                       qTh[:, br * 16 + h, :],
                                       kvh[:, h, br, :], True, False,
                                       waits=lw, fd=65)
                            kb.mm(bank[:, br * 65:br * 65 + 65],
                                  qTh[:, br * 16 + h, :], kvl[:, h, br, :],
                                  False, False, fd=65)
                            m3 = kb.mm(bank[:, br * 65:br * 65 + 65],
                                       qTl[:, br * 16 + h, :],
                                       kvh[:, h, br, :], False, True, fd=65)
                            toks_mm.append(m3)
                        tev = kb.act(asb[:, h, :], bank[:, 0:130], AF.Copy,
                                     waits=toks_mm, fd=130)
                        mm_prev[h % 2] = tev
                    t_z0 = kb.vop("ts", z0[:, :], asb[:, :, 64], kb.cb("eps6"),
                                  None, OP.max, waits=[mm_prev[0], mm_prev[1]],
                                  fd=16, eng="gpsimd")
                    t_z1 = kb.vop("ts", z1[:, :], asb[:, :, 129],
                                  kb.cb("eps6"), None, OP.max,
                                  waits=[mm_prev[0], mm_prev[1]], fd=16,
                                  eng="gpsimd")
                    t_r0 = kb.vop("recip", z0r[:, :], z0[:, :], waits=[t_z0],
                                  fd=16)
                    t_r1 = kb.vop("recip", z1r[:, :], z1[:, :], waits=[t_z1],
                                  fd=16)
                    t_b0 = kb.vop("copy", z0rb[:, :], z0r[:, :], waits=[t_r0],
                                  fd=16, eng="gpsimd")
                    t_b1 = kb.vop("copy", z1rb[:, :], z1r[:, :], waits=[t_r1],
                                  fd=16, eng="gpsimd")
                    t_a0 = kb.vop("tt",
                                  at0[:, :].rearrange("p (h d) -> p h d", d=64),
                                  asb[:, :, 0:64],
                                  z0rb[:, :].rearrange("p (h o) -> p h o", o=1)
                                  .broadcast_to([128, 16, 64]), OP.mult,
                                  waits=[t_b0], fd=1024)
                    t_at = None
                    for hh in range(H):
                        t_at = kb.vop(
                            "stt", attn[:, hh * 64:(hh + 1) * 64],
                            asb[:, hh, 65:129], z1rb[:, hh:hh + 1],
                            at0[:, hh * 64:(hh + 1) * 64],
                            OP.mult, OP.add,
                            waits=([t_b1, t_a0, prev_od] if hh == 0 else ()),
                            fd=64)
                    t_sp = kb.split(athi[:, :], atlo[:, :], attn[:, :],
                                    waits=[t_at], fd=1024)
                    tps = []
                    for e in range(NE):
                        tps.append(kb.tp(ptA[:, e * 128:(e + 1) * 128],
                                         athi[:, e * 128:(e + 1) * 128],
                                         waits=([t_sp] if e == 0 else ())))
                    for e in range(NE):
                        tps.append(kb.tp(ptB[:, e * 128:(e + 1) * 128],
                                         atlo[:, e * 128:(e + 1) * 128],
                                         waits=([t_sp] if e == 0 else ())))
                    tev1 = kb.act(atThi[:, :, :], ptA[:, :].rearrange(
                        "p (c n) -> p c n", n=128), AF.Copy,
                        waits=[tps[NE - 1]], fd=1024)
                    tev2 = kb.act(atTlo[:, :, :], ptB[:, :].rearrange(
                        "p (c n) -> p c n", n=128), AF.Copy, waits=[tps[-1]],
                        fd=1024)
                    to_ = dense([pb[0], pb[1]], (atThi, atTlo), wo[:2],
                                bo[:2], waits=[tev1, tev2])
                    te1 = kb.act(proj[:, 0:512], pb[0][:, :], AF.Copy,
                                 waits=[to_[0]], fd=512)
                    te2 = kb.act(proj[:, 512:1024], pb[1][:, :], AF.Copy,
                                 waits=[to_[1]], fd=512)
                    t_lnp = kb.ln(attn, proj[:, :], sc, waits=[te1, te2])
                    prev_od = kb.dma("gpsimd",
                                     attnln_r[bp, j * 128:(j + 1) * 128, :],
                                     attn[:, :], waits=[t_lnp])
            P.barrier()

        if nblocks >= 2:
            emit_block2(kb, pb, ptA, ptB, sc, wnames, xb, attnln_d, stream_d,
                        stream2_d, lnT_d, qT_d, kT_d, v_d, load_w, load_bias,
                        dense, prep_lnT, iota16, ones1, wstage)
        if nblocks >= 3:
            emit_block3(kb, pb, ptA, ptB, sc, wnames, stream_d, stream2_d,
                        qT_d, kT_d, v_d, load_w, load_bias, dense, prep_lnT,
                        ones1)
        if nblocks >= 4:
            emit_block4(kb, pb, ptA, ptB, sc, wnames, stream_d, lnT_d, yqT_d,
                        outh0_d, out_d, load_w, load_bias, prep_lnT, ones1,
                        wstage)
        else:
            srcmap = {1: attnln_d, 2: stream2_d, 3: stream_d}
            src = srcmap.get(min(nblocks, 3), attnln_d)
            if nblocks == 1:
                with contextlib.ExitStack() as ph:
                    xt = kb.sb("xO", [128, 1024], F32, ph)
                    at = kb.sb("aO", [128, 1024], F32, ph)
                    x1 = kb.sb("x1O", [128, 1024], F32, ph)
                    pu = None
                    for t in range(NT):
                        t1 = kb.dma("sync", xt[:, :],
                                    xb[t * 128:(t + 1) * 128, :], waits=[pu])
                        t2 = kb.dma("sync", at[:, :],
                                    attnln_d[t * 128:(t + 1) * 128, :],
                                    waits=[pu])
                        t3 = kb.vop("tt", x1[:, :], xt[:, :], at[:, :],
                                    OP.add, waits=[t1, t2], fd=1024)
                        pu = kb.dma("gpsimd", out_d[t * 128:(t + 1) * 128, :],
                                    x1[:, :], waits=[t3])
            else:
                with contextlib.ExitStack() as ph:
                    x1 = kb.sb("xO2", [128, 1024], F32, ph)
                    pu = None
                    for t in range(NT):
                        t1 = kb.dma("sync", x1[:, :],
                                    src[t * 128:(t + 1) * 128, :], waits=[pu])
                        pu = kb.dma("gpsimd", out_d[t * 128:(t + 1) * 128, :],
                                    x1[:, :], waits=[t1])

        P.barrier()

        P.finalize_marks()
        semstack = contextlib.ExitStack()
        sems = {}
        for e in ENGINES:
            sems["c_" + e] = semstack.enter_context(nc.semaphore(f"cs_{e}"))
            sems["d_" + e] = semstack.enter_context(nc.semaphore(f"ds_{e}"))
        nopbuf = kb.nopbuf

        def mknop(eng_name):
            def nop(eng):
                if hasattr(eng, "engine_nop"):
                    eng.engine_nop()
                else:
                    eng.activation(out=nopbuf[:, :], in_=kb.cbs[:, 0:1],
                                   func=AF.Copy)
            return nop

        with semstack, nc.Block() as block:
            for e in ENGINES:
                getattr(block, e)(P.runner(e, sems, mknop(e)))
    return nc


def emit_block2(kb, pb, ptA, ptB, sc, wnames, xb, attnln_d, stream_d,
                stream2_d, lnT_d, qT_d, kT_d, v_d, load_w, load_bias, dense,
                prep_lnT, iota16, ones1, wstage):
    P = kb.P
    nc = kb.nc

    mask_all = kb.sb("maskall", [128, NT, 48])
    for f, wname1, wname2, b1n, b2n, outT, outdst in (
            (1, "W1k", "W2k", "b1k", "b2k", True, kT_d),
            (2, "W1v", "W2v", "b1v", "b2v", False, v_d),
            (0, "W1q", "W2q", "b1q", "b2q", True, qT_d)):
        first = (f == 1)
        with contextlib.ExitStack() as ph:
            w1 = load_w(wnames[wname1], E, E, ph)
            w2 = load_w(wnames[wname2], E, E, ph)
            b1 = load_bias(wnames[b1n], E, ph)
            b2h = kb.sb("b2h", [16, E], BF16, ph)
            b2l = kb.sb("b2l", [16, E], BF16, ph)
            b2st = kb.sb("b2st", [16, E], F32, ph)
            tb2d = kb.dma("sync", b2st[:, :], wnames[b2n][:, :])
            tb2 = kb.split(b2h[:, :], b2l[:, :], b2st[:, :], waits=[tb2d],
                           fd=1024)
            if first:
                nodes = load_w(wnames["nodecat"], E, 48, ph)
                nodebh = kb.sb("ndbh", [1, 48], BF16, ph)
                nodebl = kb.sb("ndbl", [1, 48], BF16, ph)
                ndst = kb.sb("ndst", [1, 48], F32, ph)
                tnd = kb.dma("sync", ndst[:, :],
                             wnames["nodeb"].rearrange("(o n) -> o n", o=1))
                tnds = kb.split(nodebh[:, :], nodebl[:, :], ndst[:, :],
                                waits=[tnd], fd=48)
            P.barrier()

            xt = [kb.sb("xS", [128, 1024], F32, ph) for _ in range(2)]
            at = kb.sb("atS", [128, 1024], F32, ph)
            x1 = kb.sb("x1S", [128, 1024], F32, ph)
            lnf = kb.sb("lnS", [128, 1024], F32, ph)
            lnhi = kb.sb("lnSh", [128, 1024], BF16, ph)
            lnlo = kb.sb("lnSl", [128, 1024], BF16, ph)
            lnThi = kb.sb("lnSTh", [128, NE, 128], BF16, ph)
            lnTlo = kb.sb("lnSTl", [128, NE, 128], BF16, ph)
            scor = kb.sb("scor", [128, 48], F32, ph)
            hid = kb.sb("hid", [128, 1024], F32, ph)
            hidm = kb.sb("hidm", [128, 1024], F32, ph)
            hmh = kb.sb("hmh", [128, 1024], BF16, ph)
            hml = kb.sb("hml", [128, 1024], BF16, ph)
            hTh = kb.sb("hTh", [128, NE, 128], BF16, ph)
            hTl = kb.sb("hTl", [128, NE, 128], BF16, ph)
            ffo = kb.sb("ffo", [128, 1024], F32, ph)
            foh = kb.sb("foh", [128, 1024], BF16, ph)
            fol = kb.sb("fol", [128, 1024], BF16, ph)
            foTh = kb.sb("foTh", [128, NE, 128], BF16, ph)
            foTl = kb.sb("foTl", [128, NE, 128], BF16, ph)
            m16b = kb.sb("m16b", [128, 16], BF16, ph)
            m16T = kb.sb("m16T", [16, 128], BF16, ph)
            rt = {n: kb.sb(n, [128, 24], F32, ph) for n in
                  ("oh", "prod", "prodb", "scur", "bbit", "idx", "idx2",
                   "leaf")}

            lnT_r = [d.rearrange("c p t -> p c t") for d in lnT_d]
            outT_r = [d.rearrange("c p t -> p c t") for d in outdst] \
                if outT else None

            prev_use = [None, None]
            t_msk_done = None
            prev_tr = [None, None]
            prev_d = [None, None]
            prev_lnTd = [None, None]
            prev_tmm = None
            prev_fsp = None
            prev_th = []
            _pd_scor = [None]
            _scord = None
            _pd_s0 = [None, None, None]
            _s0d = None
            prev_scor_rd = []
            prev_x1 = []
            prev_at = None
            if first and "s0stage" in kb._dbg:
                _s0d = kb.nc.dram_tensor("s0stage", [NT, 3, 128, 1024], F32,
                                         kind="ExternalOutput").ap()
            if first and "s0dbg" in kb._dbg:
                _scord = kb.nc.dram_tensor("scordbg", [NT, 128, 48], F32,
                                           kind="ExternalOutput").ap()
                _maskd = kb.nc.dram_tensor("maskdbg", [128, NT * 48], F32,
                                           kind="ExternalOutput").ap()
            for t in range(NT):
                par = t % 2
                if first:
                    t1 = kb.dma("sync", xt[par][:, :],
                                xb[t * 128:(t + 1) * 128, :],
                                waits=[prev_use[par]])
                    t2 = kb.dma("sync", at[:, :],
                                attnln_d[t * 128:(t + 1) * 128, :],
                                waits=[prev_use[par], prev_at])
                    t3 = kb.vop("tt", x1[:, :], xt[par][:, :], at[:, :],
                                OP.add, waits=[t1, t2] + prev_x1, fd=1024)
                    prev_at = t3
                    t_sdma = kb.dma("gpsimd",
                                    stream_d[t * 128:(t + 1) * 128, :],
                                    x1[:, :], waits=[t3])
                    t_ln, t_lnT = prep_lnT(x1[:, :], lnf, lnhi, lnlo, lnThi,
                                           lnTlo, waits=[t3], pt=(ptA, ptB),
                                           guard2=prev_lnTd)
                    prev_use[par] = t_ln
                    prev_x1 = [t_ln, t_sdma]
                    prev_lnTd[0] = kb.dma(
                        "gpsimd", lnT_r[0][:, :, t * 128:(t + 1) * 128],
                        lnThi[:, :, :], waits=[t_lnT[0]])
                    prev_lnTd[1] = kb.dma(
                        "gpsimd", lnT_r[1][:, :, t * 128:(t + 1) * 128],
                        lnTlo[:, :, :], waits=[t_lnT[1]])
                    i = 0
                    n_mm = NE * 3 + 2
                    for e in range(NE):
                        for (a_, b_) in ((0, 0), (0, 1), (1, 0)):
                            tokm = kb.mm(pb[4][:, 0:48],
                                         (lnThi, lnTlo)[a_][:, e, :],
                                         nodes[b_][:, e, :],
                                         start=(i == 0), stop=False,
                                         waits=(t_lnT + [tnds]
                                                if i == 0 else ()), fd=48)
                            i += 1
                    kb.mm(pb[4][:, 0:48], ones1[:, :], nodebh[:, :],
                          False, False, fd=48)
                    tokm = kb.mm(pb[4][:, 0:48], ones1[:, :], nodebl[:, :],
                                 False, True, fd=48)
                    t_sc = kb.act(scor[:, :], pb[4][:, 0:48], AF.Copy,
                                  waits=[tokm, _pd_scor[0]] + prev_scor_rd,
                                  fd=48)
                    if _scord is not None:
                        _pd_scor[0] = kb.dma("sync", _scord[t, :, :],
                                             scor[:, :], waits=[t_sc])
                    t_idx = {}
                    for ff in range(3):
                        tb0 = kb.vop("ts", rt["bbit"][:, ff:ff + 1],
                                     scor[:, ff * 16:ff * 16 + 1],
                                     kb.cb("zero"), None, OP.is_ge,
                                     waits=[t_sc], fd=1)
                        t_idx[ff] = kb.act(rt["idx"][:, ff:ff + 1],
                                           rt["bbit"][:, ff:ff + 1],
                                           AF.Copy, bias=1.0,
                                           waits=[tb0], fd=1)
                    for lev in (1, 2, 3):
                        lo_, n_ = 2 ** lev - 1, 2 ** lev
                        for ff in range(3):
                            c0 = ff * 16 + lo_
                            s0 = ff * 8
                            t_oh = kb.vop("ts", rt["oh"][:, s0:s0 + n_],
                                          iota16[:, lo_:lo_ + n_],
                                          rt["idx"][:, ff:ff + 1], None,
                                          OP.is_equal, waits=[t_idx[ff]],
                                          fd=n_, eng="gpsimd")
                            t_pr = kb.vop("tt", rt["prod"][:, s0:s0 + n_],
                                          rt["oh"][:, s0:s0 + n_],
                                          scor[:, c0:c0 + n_], OP.mult,
                                          waits=[t_oh], fd=n_)
                            t_pb = kb.vop("copy", rt["prodb"][:, s0:s0 + n_],
                                          rt["prod"][:, s0:s0 + n_],
                                          waits=[t_pr], fd=n_, eng="gpsimd")
                            t_sr = kb.vop("red", rt["scur"][:, ff:ff + 1],
                                          rt["prodb"][:, s0:s0 + n_],
                                          AX.X, OP.add, False,
                                          waits=[t_pb], fd=1)
                            t_bb = kb.vop("ts", rt["bbit"][:, ff:ff + 1],
                                          rt["scur"][:, ff:ff + 1],
                                          kb.cb("zero"), None, OP.is_ge,
                                          waits=[t_sr], fd=1, eng="gpsimd")
                            t_i2 = kb.act(rt["idx2"][:, ff:ff + 1],
                                          rt["idx"][:, ff:ff + 1], AF.Copy,
                                          scale=2.0, bias=1.0,
                                          waits=[t_idx[ff]], fd=1)
                            t_idx[ff] = kb.vop("tt", rt["idx"][:, ff:ff + 1],
                                               rt["idx2"][:, ff:ff + 1],
                                               rt["bbit"][:, ff:ff + 1],
                                               OP.add, waits=[t_i2, t_bb],
                                               fd=1)
                    prev_scor_rd = [t_pr]
                    tm = []
                    for ff in range(3):
                        t_lfs = kb.act(rt["leaf"][:, ff:ff + 1],
                                       rt["idx"][:, ff:ff + 1], AF.Copy,
                                       bias=-15.0, waits=[t_idx[ff]], fd=1)
                        tm.append(kb.vop("ts",
                                         mask_all[:, t, ff * 16:ff * 16 + 16],
                                         iota16[:, :],
                                         rt["leaf"][:, ff:ff + 1], None,
                                         OP.is_equal, waits=[t_lfs], fd=16,
                                         eng="gpsimd"))
                    t_msk_done = tm[-1]
                    if _scord is not None and t == NT - 1:
                        kb.dma("sync", _maskd[:, :],
                               mask_all[:, :, :].rearrange(
                                   "p a b -> p (a b)"),
                               waits=tm)
                else:
                    t1h = kb.dma("sync", lnThi[:, :, :],
                                 lnT_r[0][:, :, t * 128:(t + 1) * 128],
                                 waits=[prev_use[par]] + prev_th)
                    t1l = kb.dma("sync", lnTlo[:, :, :],
                                 lnT_r[1][:, :, t * 128:(t + 1) * 128],
                                 waits=[prev_use[par]])
                    t_lnT = [t1h, t1l]
                    tm = [None]
                th = dense([pb[0], pb[1]], (lnThi, lnTlo), w1[:2], b1[:2],
                           waits=list(t_lnT) + prev_tr)
                tr1 = kb.act(hid[:, 0:512], pb[0][:, :], AF.Relu,
                             waits=[th[0], prev_tmm, _pd_s0[0]], fd=512)
                tr2 = kb.act(hid[:, 512:1024], pb[1][:, :], AF.Relu,
                             waits=[th[1]], fd=512)
                prev_tr = [tr1, tr2]
                prev_th = list(th)
                tmm = []
                for lf in range(16):
                    tmm.append(kb.vop(
                        "ts", hidm[:, lf * 64:(lf + 1) * 64],
                        hid[:, lf * 64:(lf + 1) * 64],
                        mask_all[:, t, f * 16 + lf:f * 16 + lf + 1], None,
                        OP.mult, waits=[tr1, tr2] + tm, fd=64))
                prev_tmm = tmm[-1]
                if _s0d is not None and first:
                    _pd_s0[0] = kb.dma("sync", _s0d[t, 0, :, :], hid[:, :],
                                       waits=[tr1, tr2])
                    _pd_s0[1] = kb.dma("sync", _s0d[t, 1, :, :], hidm[:, :],
                                       waits=tmm + [_pd_s0[1]])
                t_hsp = kb.split(hmh[:, :], hml[:, :], hidm[:, :],
                                 waits=tmm + [_pd_s0[1]], fd=1024)
                tps = []
                for e in range(NE):
                    tps.append(kb.tp(ptA[:, e * 128:(e + 1) * 128],
                                     hmh[:, e * 128:(e + 1) * 128],
                                     waits=([t_hsp] if e == 0 else ())))
                for e in range(NE):
                    tps.append(kb.tp(ptB[:, e * 128:(e + 1) * 128],
                                     hml[:, e * 128:(e + 1) * 128],
                                     waits=([t_hsp] if e == 0 else ())))
                tev1 = kb.act(hTh[:, :, :], ptA[:, :].rearrange(
                    "p (c n) -> p c n", n=128), AF.Copy, waits=[tps[NE - 1]],
                    fd=1024)
                tev2 = kb.act(hTl[:, :, :], ptB[:, :].rearrange(
                    "p (c n) -> p c n", n=128), AF.Copy, waits=[tps[-1]],
                    fd=1024)
                t_mb = kb.vop("copy", m16b[:, :],
                              mask_all[:, t, f * 16:f * 16 + 16],
                              waits=tm, fd=16)
                t_mt = kb.tp(ptA[0:16, 0:128], m16b[:, :],
                             waits=[t_mb, tev1])
                t_me = kb.act(m16T[:, :], ptA[0:16, 0:128], AF.Copy,
                              waits=[t_mt], fd=128)
                toks2 = []
                for hhalf in range(2):
                    cs, ce = hhalf * 512, (hhalf + 1) * 512
                    i = 0
                    n_mm = NE * 3 + 4
                    tok = None
                    for e in range(NE):
                        for (a_, b_) in ((0, 0), (0, 1), (1, 0)):
                            tok = kb.mm(pb[2 + hhalf][:, :],
                                        (hTh, hTl)[a_][:, e, :],
                                        w2[b_][:, e, cs:ce],
                                        start=(i == 0), stop=False,
                                        waits=([tev1, tev2, t_me]
                                               if i == 0 else ()), fd=512)
                            i += 1
                    kb.mm(pb[2 + hhalf][:, :], m16T[:, :], b2h[:, cs:ce],
                          False, False, waits=[tb2], fd=512)
                    tok = kb.mm(pb[2 + hhalf][:, :], m16T[:, :],
                                b2l[:, cs:ce], False, True, fd=512)
                    toks2.append(tok)
                te1 = kb.act(ffo[:, 0:512], pb[2][:, :], AF.Copy,
                             waits=[toks2[0], prev_fsp, _pd_s0[2]], fd=512)
                te2 = kb.act(ffo[:, 512:1024], pb[3][:, :], AF.Copy,
                             waits=[toks2[1]], fd=512)
                if _s0d is not None and first:
                    _pd_s0[2] = kb.dma("sync", _s0d[t, 2, :, :], ffo[:, :],
                                       waits=[te1, te2])
                t_fsp = kb.split(foh[:, :], fol[:, :], ffo[:, :],
                                 waits=[te1, te2] + prev_d, fd=1024)
                prev_fsp = t_fsp
                if outT:
                    tps = []
                    for e in range(NE):
                        tps.append(kb.tp(ptA[:, e * 128:(e + 1) * 128],
                                         foh[:, e * 128:(e + 1) * 128],
                                         waits=([t_fsp] if e == 0 else ())))
                    for e in range(NE):
                        tps.append(kb.tp(ptB[:, e * 128:(e + 1) * 128],
                                         fol[:, e * 128:(e + 1) * 128],
                                         waits=([t_fsp] if e == 0 else ())))
                    tv1 = kb.act(foTh[:, :, :], ptA[:, :].rearrange(
                        "p (c n) -> p c n", n=128), AF.Copy,
                        waits=[tps[NE - 1], prev_d[0]], fd=1024)
                    tv2 = kb.act(foTl[:, :, :], ptB[:, :].rearrange(
                        "p (c n) -> p c n", n=128), AF.Copy,
                        waits=[tps[-1], prev_d[1]], fd=1024)
                    d1 = kb.dma("gpsimd",
                                outT_r[0][:, :, t * 128:(t + 1) * 128],
                                foTh[:, :, :], waits=[tv1])
                    d2 = kb.dma("gpsimd",
                                outT_r[1][:, :, t * 128:(t + 1) * 128],
                                foTl[:, :, :], waits=[tv2])
                    prev_d = [d1, d2]
                    if not first:
                        prev_use[par] = tv2
                else:
                    d1 = kb.dma("gpsimd", v_d[0][t * 128:(t + 1) * 128, :],
                                foh[:, :], waits=[t_fsp])
                    d2 = kb.dma("gpsimd", v_d[1][t * 128:(t + 1) * 128, :],
                                fol[:, :], waits=[t_fsp])
                    prev_d = [d1, d2]
                    prev_use[par] = t_fsp
            P.barrier()

    emit_flash(kb, pb, ptA, ptB, sc, qT_d, kT_d, v_d, stream_d, stream2_d,
               causal=False, wo=None, bo=None, ones1=ones1, prep=None)


def emit_block3(kb, pb, ptA, ptB, sc, wnames, stream_d, stream2_d, qT_d,
                kT_d, v_d, load_w, load_bias, dense, prep_lnT, ones1):
    P = kb.P

    with contextlib.ExitStack() as ph:
        wq = load_w(wnames["L1WqT"], E, E, ph)
        wk = load_w(wnames["L1WkT"], E, E, ph)
        wv = load_w(wnames["L1WvT"], E, E, ph)
        P.barrier()

        xt = [kb.sb("x3", [128, 1024], F32, ph) for _ in range(2)]
        lnf = kb.sb("ln3", [128, 1024], F32, ph)
        lnhi = kb.sb("ln3h", [128, 1024], BF16, ph)
        lnlo = kb.sb("ln3l", [128, 1024], BF16, ph)
        lnThi = kb.sb("ln3Th", [128, NE, 128], BF16, ph)
        lnTlo = kb.sb("ln3Tl", [128, NE, 128], BF16, ph)
        osb = kb.sb("osb3", [128, 1024], F32, ph)
        oh = kb.sb("oh3", [128, 1024], BF16, ph)
        ol = kb.sb("ol3", [128, 1024], BF16, ph)
        oTh = kb.sb("oTh3", [128, NE, 128], BF16, ph)
        oTl = kb.sb("oTl3", [128, NE, 128], BF16, ph)

        qT_r = [d.rearrange("c p t -> p c t") for d in qT_d]
        kT_r = [d.rearrange("c p t -> p c t") for d in kT_d]

        prev_use = [None, None]
        prev_sp = None
        prev_te = {0: [], 1: []}
        prev_dT = []
        prev_dV = []
        for t in range(NT):
            par = t % 2
            t1 = kb.dma("sync", xt[par][:, :],
                        stream2_d[t * 128:(t + 1) * 128, :],
                        waits=[prev_use[par]])
            t_ln, t_lnT = prep_lnT(xt[par][:, :], lnf, lnhi, lnlo, lnThi,
                                   lnTlo, waits=[t1], pt=(ptA, ptB))
            prev_use[par] = t_ln
            for wi, (w_, dstT, dstv) in enumerate(
                    ((wq, qT_r, None), (wk, kT_r, None), (wv, None, v_d))):
                bp_ = wi % 2
                tp_ = dense([pb[2 * bp_], pb[2 * bp_ + 1]],
                            (lnThi, lnTlo), w_[:2], None,
                            waits=list(t_lnT) + prev_te[bp_])
                te1 = kb.act(osb[:, 0:512], pb[2 * bp_][:, :], AF.Copy,
                             waits=[tp_[0], prev_sp], fd=512)
                te2 = kb.act(osb[:, 512:1024], pb[2 * bp_ + 1][:, :],
                             AF.Copy, waits=[tp_[1]], fd=512)
                prev_te[bp_] = [te1, te2]
                t_sp = kb.split(oh[:, :], ol[:, :], osb[:, :],
                                waits=[te1, te2] + prev_dV, fd=1024)
                prev_sp = t_sp
                if dstT is not None:
                    tps = []
                    for e in range(NE):
                        tps.append(kb.tp(ptA[:, e * 128:(e + 1) * 128],
                                         oh[:, e * 128:(e + 1) * 128],
                                         waits=([t_sp] if e == 0 else ())))
                    for e in range(NE):
                        tps.append(kb.tp(ptB[:, e * 128:(e + 1) * 128],
                                         ol[:, e * 128:(e + 1) * 128],
                                         waits=([t_sp] if e == 0 else ())))
                    tv1 = kb.act(oTh[:, :, :], ptA[:, :].rearrange(
                        "p (c n) -> p c n", n=128), AF.Copy,
                        waits=[tps[NE - 1]] + prev_dT, fd=1024)
                    tv2 = kb.act(oTl[:, :, :], ptB[:, :].rearrange(
                        "p (c n) -> p c n", n=128), AF.Copy,
                        waits=[tps[-1]], fd=1024)
                    d1 = kb.dma("gpsimd",
                                dstT[0][:, :, t * 128:(t + 1) * 128],
                                oTh[:, :, :], waits=[tv1])
                    d2 = kb.dma("gpsimd",
                                dstT[1][:, :, t * 128:(t + 1) * 128],
                                oTl[:, :, :], waits=[tv2])
                    prev_dT = [d1, d2]
                else:
                    d1 = kb.dma("gpsimd", dstv[0][t * 128:(t + 1) * 128, :],
                                oh[:, :], waits=[t_sp])
                    d2 = kb.dma("gpsimd", dstv[1][t * 128:(t + 1) * 128, :],
                                ol[:, :], waits=[t_sp])
                    prev_dV = [d1, d2]
        P.barrier()

    with contextlib.ExitStack() as ph:
        wo = load_w(wnames["L1WoT"], E, E, ph)
        P.barrier()
        emit_flash(kb, pb, ptA, ptB, sc, qT_d, kT_d, v_d, stream2_d,
                   stream_d, causal=True, wo=wo, bo=None, ones1=ones1,
                   prep=None)


def kb_wo_ap(kb):
    return kb._woT_ap


def emit_flash(kb, pb, ptA, ptB, sc, qT_d, kT_d, v_d, resid_d, outstream_d,
               causal, wo, bo, ones1, prep):
    P = kb.P
    nc = kb.nc
    G = 8
    INV32 = 1.0 / 32.0

    with contextlib.ExitStack() as ph:
        qTg = [kb.sb("qTg", [128, NE, 1024], BF16, ph) for _ in range(2)]
        kTb = [kb.sb("kTbF", [128, NE, 512], BF16, ph) for _ in range(2)]
        vbb = [kb.sb("vbbF", [128, 4, 1024], BF16, ph) for _ in range(2)]
        acc = kb.sb("accF", [128, G, 1024], F32, ph)
        marr = kb.sb("marrF", [128, G], F32, ph)
        nbarr = kb.sb("nbarrF", [128, G], F32, ph)
        larr = kb.sb("larrF", [128, G], F32, ph)
        bm = kb.sb("bmF", [128, 1], F32, ph)
        lb = kb.sb("lbF", [128, 1], F32, ph)
        linv = kb.sb("linvF", [128, 1], F32, ph)
        linb = kb.sb("linbF", [128, 1], F32, ph)
        ssb = kb.sb("ssbF", [128, 512], F32, ph)
        psb = kb.sb("psbF", [128, 512], F32, ph)
        ph_ = kb.sb("phF", [128, 512], BF16, ph)
        pl_ = kb.sb("plF", [128, 512], BF16, ph)
        pth = kb.sb("pthF", [128, 512], BF16, ph)
        ptl = kb.sb("ptlF", [128, 512], BF16, ph)
        rs = kb.sb("rsF", [128, 1], F32, ph)
        attnb = kb.sb("attnbF", [128, 1024], F32, ph)
        xres = kb.sb("xresF", [128, 1024], F32, ph)
        xout = kb.sb("xoutF", [128, 1024], F32, ph)
        lnat = kb.sb("lnatF", [128, 1024], F32, ph)
        if wo is not None:
            athi = kb.sb("athF", [128, 1024], BF16, ph)
            atlo = kb.sb("atlF", [128, 1024], BF16, ph)
            atThi = kb.sb("atThF", [128, NE, 128], BF16, ph)
            atTlo = kb.sb("atTlF", [128, NE, 128], BF16, ph)
            proj = kb.sb("projFF", [128, 1024], F32, ph)
        if causal:
            amask = kb.sb("amaskF", [128, 4, 512], F32, ph)
            ki = kb.sb("kiF", [128, 512], F32, ph)
            qp = kb.sb("qpF", [128, 1], F32, ph)
            thr = kb.sb("thrF", [128, 4], F32, ph)
            t_ki = kb.iota(ki[:, :], [[1, 512]], 0, 0, fd=512)
            t_qp = kb.iota(qp[:, :], [[1, 1]], 0, 1, fd=1)
            prevm = None
            for r in range(4):
                t_th = kb.act(thr[:, r:r + 1], qp[:, :], AF.Copy,
                              bias=float(128 * r), waits=[t_qp], fd=1)
                t_m1 = kb.vop("ts", ssb[:, :], ki[:, :], thr[:, r:r + 1],
                              None, OP.is_gt, waits=[t_th, t_ki, prevm],
                              fd=512)
                t_m2 = kb.vop("ts", amask[:, r, :], ssb[:, :],
                              kb.cb("negbig"), None, OP.mult,
                              waits=[t_m1], fd=512)
                prevm = t_m2
        else:
            prevm = None

        qT_r = [d.rearrange("c p t -> p c t") for d in qT_d]
        kT_r = [d.rearrange("c p t -> p c t") for d in kT_d]
        v_r = [d.rearrange("(c p) m -> p c m", p=128) for d in v_d]

    
        sguard = [[], []]
        pvguard = [[], []]
        g_split = None
        g_tp = None
        g_ptev = []
        g_pv = [None, None]
        g_rsl = None
        g_mn = None
        g_nb = {}
        g_qload = []
        g_kload = []
        g_vload = []
        g_xo = None
        g_od = None
        g_at = {}
        g_woev = [None, None]
        g_lbg = None
        g_lnprev = None
        g_ssb = None
        t_ss = prevm

        for g in range(4):
            tq0 = kb.dma("sync", qTg[0][:, :, :],
                         qT_r[0][:, :, g * 1024:(g + 1) * 1024],
                         waits=g_qload)
            tq1 = kb.dma("sync", qTg[1][:, :, :],
                         qT_r[1][:, :, g * 1024:(g + 1) * 1024])
            cnt = [0] * G
            nkb = (2 * g + 2) if causal else NT // 4
            last_smm = None
            for j in range(nkb):
                tk1 = kb.dma("sync", kTb[0][:, :, :],
                             kT_r[0][:, :, j * 512:(j + 1) * 512],
                             waits=g_kload)
                tk2 = kb.dma("sync", kTb[1][:, :, :],
                             kT_r[1][:, :, j * 512:(j + 1) * 512])
                tv1 = kb.dma("sync", vbb[0][:, :, :],
                             v_r[0][:, 4 * j:4 * j + 4, :], waits=g_vload)
                tv2 = kb.dma("sync", vbb[1][:, :, :],
                             v_r[1][:, 4 * j:4 * j + 4, :])
                for i in range(G):
                    t = g * 8 + i
                    if causal and j > t // 4:
                        continue
                    first = (cnt[i] == 0)
                    cnt[i] += 1
                    diag = causal and (j == t // 4)
                    sb_i = (i + j) % 2
                    bank = pb[sb_i]
                    smm = None
                    nmm = NE * 3
                    idx = 0
                    for e in range(NE):
                        for (a, b) in ((0, 0), (0, 1), (1, 0)):
                            w0 = (sguard[sb_i] + [tq1, tk2]) if idx == 0 \
                                else ()
                            smm = kb.mm(bank[:, 0:512],
                                        qTg[a][:, e, i * 128:(i + 1) * 128],
                                        kTb[b][:, e, :],
                                        start=(idx == 0),
                                        stop=(idx == nmm - 1),
                                        waits=w0, fd=512)
                            idx += 1
                    last_smm = smm
                    if diag:
                        r = t % 4
                        t_ss = kb.vop("tt", ssb[:, :], bank[:, 0:512],
                                      amask[:, r, :], OP.add,
                                      waits=[smm, t_ss, g_ssb], fd=512)
                        src_s, src_tok = ssb[:, :], t_ss
                    else:
                        src_s, src_tok = bank[:, 0:512], smm
                    if first:
                        t_bm = kb.vop("red", bm[:, :], src_s, AX.X, OP.max,
                                      False, waits=[src_tok, g_mn], fd=1)
                        t_mn = kb.vop("copy", marr[:, i:i + 1], bm[:, :],
                                      waits=[t_bm, g_nb.get(i)], fd=1,
                                      eng="gpsimd")
                        g_mn = t_mn
                        t_nb = kb.act(nbarr[:, i:i + 1], marr[:, i:i + 1],
                                      AF.Copy, scale=-INV32, waits=[t_mn],
                                      fd=1)
                        g_nb[i] = t_nb
                    t_p = kb.act(psb[:, :], src_s, AF.Exp, scale=INV32,
                                 bias=nbarr[:, i:i + 1], accum_out=rs[:, :],
                                 waits=[src_tok, g_split, g_rsl, g_nb[i]],
                                 fd=512)
                    if diag:
                        g_ssb = t_p
                    sguard[sb_i] = [t_p] if diag else \
                        ([t_p, t_bm] if first else [t_p])
                    if diag:
                        sguard[sb_i] = [t_ss]
                    if first:
                        t_l = kb.vop("copy", larr[:, i:i + 1], rs[:, :],
                                     waits=[t_p, g_lbg], fd=1)
                    else:
                        t_l = kb.vop("tt", larr[:, i:i + 1], larr[:, i:i + 1],
                                     rs[:, :], OP.add, waits=[t_p], fd=1)
                    g_rsl = t_l
                    t_sp = kb.split(ph_[:, :], pl_[:, :], psb[:, :],
                                    waits=[t_p, g_tp], fd=512)
                    g_split = t_sp
                    tps = []
                    for s in range(4):
                        tps.append(kb.tp(ptA[:, s * 128:(s + 1) * 128],
                                         ph_[:, s * 128:(s + 1) * 128],
                                         waits=([t_sp] + g_ptev) if s == 0
                                         else ()))
                    for s in range(4):
                        tps.append(kb.tp(ptB[:, s * 128:(s + 1) * 128],
                                         pl_[:, s * 128:(s + 1) * 128]))
                    g_tp = tps[-1]
                    t_the = kb.act(pth[:, :], ptA[:, 0:512], AF.Copy,
                                   waits=[tps[3], g_pv[0], g_pv[1]], fd=512)
                    t_tle = kb.act(ptl[:, :], ptB[:, 0:512], AF.Copy,
                                   waits=[tps[7]], fd=512)
                    g_ptev = [t_the, t_tle]
                    for half in range(2):
                        cs, ce = half * 512, (half + 1) * 512
                        idx = 0
                        stop_tok = None
                        for s in range(4):
                            for (a, b) in ((0, 0), (0, 1), (1, 0)):
                                pT = (pth, ptl)[a]
                                w0 = ([t_the, t_tle, tv2] + pvguard[half]) \
                                    if idx == 0 else ()
                                stop_tok = kb.mm(
                                    pb[2 + half][:, 0:512],
                                    pT[:, s * 128:(s + 1) * 128],
                                    vbb[b][:, s, cs:ce],
                                    start=(idx == 0), stop=(idx == 11),
                                    waits=w0, fd=512)
                                idx += 1
                        g_pv[half] = stop_tok
                        tgt = acc[:, i, cs:ce]
                        if first:
                            t_a = kb.vop("copy", tgt, pb[2 + half][:, 0:512],
                                         waits=[stop_tok, g_at.get(i)],
                                         fd=512)
                        else:
                            t_a = kb.vop("tt", tgt, tgt,
                                         pb[2 + half][:, 0:512], OP.add,
                                         waits=[stop_tok], fd=512)
                        pvguard[half] = [t_a]
                g_kload = [last_smm]
                g_vload = [g_pv[0], g_pv[1]]
            g_qload = [last_smm]

            for i in range(G):
                t = g * 8 + i
                t_lb = kb.vop("copy", lb[:, :], larr[:, i:i + 1],
                              waits=[g_rsl], fd=1, eng="gpsimd")
                g_lbg = t_lb
                t_li = kb.vop("recip", linv[:, :], lb[:, :], waits=[t_lb],
                              fd=1)
                t_lib = kb.vop("copy", linb[:, :], linv[:, :], waits=[t_li],
                               fd=1, eng="gpsimd")
                t_at = kb.vop("ts", attnb[:, :],
                              acc[:, i, :], linb[:, :], None, OP.mult,
                              waits=[t_lib] + pvguard[0] + pvguard[1],
                              fd=1024)
                g_at[i] = t_at
                if wo is None:
                    t_ln = kb.ln(lnat, attnb[:, :], sc, waits=[t_at])
                else:
                    t_spA = kb.split(athi[:, :], atlo[:, :], attnb[:, :],
                                     waits=[t_at, g_tp], fd=1024)
                    tpsA = []
                    for e in range(NE):
                        tpsA.append(kb.tp(ptA[:, e * 128:(e + 1) * 128],
                                          athi[:, e * 128:(e + 1) * 128],
                                          waits=([t_spA] + g_ptev)
                                          if e == 0 else ()))
                    for e in range(NE):
                        tpsA.append(kb.tp(ptB[:, e * 128:(e + 1) * 128],
                                          atlo[:, e * 128:(e + 1) * 128]))
                    g_tp = tpsA[-1]
                    tevA = kb.act(atThi[:, :, :], ptA[:, :].rearrange(
                        "p (c n) -> p c n", n=128), AF.Copy,
                        waits=[tpsA[NE - 1]], fd=1024)
                    tevB = kb.act(atTlo[:, :, :], ptB[:, :].rearrange(
                        "p (c n) -> p c n", n=128), AF.Copy,
                        waits=[tpsA[-1]], fd=1024)
                    g_ptev = [tevA, tevB]
                    ptoks = []
                    for half in range(2):
                        cs, ce = half * 512, (half + 1) * 512
                        idx = 0
                        tok = None
                        for e in range(NE):
                            for (a, b) in ((0, 0), (0, 1), (1, 0)):
                                w0 = ([tevA, tevB, g_woev[half]]
                                      if idx == 0 else ())
                                tok = kb.mm(pb[4 + half][:, 0:512],
                                            ((atThi, atTlo)[a])[:, e, :],
                                            wo[b][:, e, cs:ce],
                                            start=(idx == 0),
                                            stop=(idx == NE * 3 - 1),
                                            waits=w0, fd=512)
                                idx += 1
                        ptoks.append(tok)
                    te1 = kb.act(proj[:, 0:512], pb[4][:, :], AF.Copy,
                                 waits=[ptoks[0], g_lnprev], fd=512)
                    te2 = kb.act(proj[:, 512:1024], pb[5][:, :], AF.Copy,
                                 waits=[ptoks[1]], fd=512)
                    g_woev = [te1, te2]
                    t_ln = kb.ln(lnat, proj[:, :], sc, waits=[te1, te2])
                g_lnprev = t_ln
                t_xr = kb.dma("sync", xres[:, :],
                              resid_d[t * 128:(t + 1) * 128, :],
                              waits=[g_xo])
                t_xo = kb.vop("tt", xout[:, :], xres[:, :], lnat[:, :],
                              OP.add, waits=[t_ln, t_xr, g_od], fd=1024)
                g_xo = t_xo
                g_od = kb.dma("gpsimd",
                              outstream_d[t * 128:(t + 1) * 128, :],
                              xout[:, :], waits=[t_xo])
    P.barrier()


def emit_block4(kb, pb, ptA, ptB, sc, wnames, stream_d, lnT_d, yqT_d,
                outh0_d, out_d, load_w, load_bias, prep_lnT, ones1, wstage):
    P = kb.P
    nc = kb.nc
    C52L = 6755399441055744.0
    C32L = 12582912.0
    ISQ2 = 0.7071067811865476

    def fix8(kbv, dst, x, S, waits, eng_r=("vector", "vector")):
        e1, e2 = eng_r
        t0 = kbv.vop("ts", S["u"], x, 256.0, None, OP.mult, waits=waits,
                     fd=1024)
        tl = _rne_core(kbv, S["g"], S["u"], S, [t0])
        t2 = kbv.vop("ts", dst, S["g"], 1.0 / 256.0, None, OP.mult,
                     waits=[tl], fd=1024)
        return t2

    def _rne_core(kbv, fl, u, S, waits):
        ta = kbv.vop("ts", S["a"], u, kbv.cb("c52"), None, OP.add,
                     waits=waits, fd=1024)
        tf1 = kbv.vop("ts", S["f1"], S["a"], kbv.negc52, None, OP.add,
                      waits=[ta], fd=1024)
        tb = kbv.vop("tt", S["b"], u, S["f1"], OP.subtract, waits=[tf1],
                     fd=1024)
        tc = kbv.vop("ts", S["a"], S["b"], kbv.cb("c32"), None, OP.add,
                     waits=[tb], fd=1024)
        tf2 = kbv.vop("ts", S["f2"], S["a"], kbv.negc32, None, OP.add,
                      waits=[tc], fd=1024)
        tsq = kbv.vop("tt", S["a"], S["f2"], S["f2"], OP.mult, waits=[tf2],
                      fd=1024)
        tgt = kbv.vop("ts", S["b"], S["a"], 0.5625, None, OP.is_ge,
                      waits=[tsq], fd=1024)
        tg = kbv.vop("tt", S["a"], S["b"], S["f2"], OP.mult, waits=[tgt],
                     fd=1024)
        tfl = kbv.vop("tt", fl, S["f1"], S["a"], OP.add, waits=[tg], fd=1024)
        return tfl

    yqT_r = [d.rearrange("c p t -> p c t") for d in yqT_d]

    for nh in range(2):
        with contextlib.ExitStack() as ph:
            w1 = load_w(wnames["F1w1"][:, nh * 2048:(nh + 1) * 2048],
                        E, 2048, ph)
            b1 = load_bias(wnames["F1b1"], 4 * E, ph)
            P.barrier()

            xt = [kb.sb("x4", [128, 1024], F32, ph) for _ in range(2)]
            lnf = kb.sb("ln4", [128, 1024], F32, ph)
            lnhi = kb.sb("ln4h", [128, 1024], BF16, ph)
            lnlo = kb.sb("ln4l", [128, 1024], BF16, ph)
            lnThi = kb.sb("ln4Th", [128, NE, 128], BF16, ph)
            lnTlo = kb.sb("ln4Tl", [128, NE, 128], BF16, ph)
            hsb = kb.sb("hsb", [128, 2048], F32, ph)
            S = {n: kb.sb("g_" + n, [128, 1024], F32, ph) for n in
                 ("u", "a", "f1", "b", "f2", "g", "xq", "tt", "fi", "xg",
                  "fr", "e0", "y0", "y1")}
            hqh = kb.sb("hqh", [128, 2048], BF16, ph)
            hql = kb.sb("hql", [128, 2048], BF16, ph)
            hTh = kb.sb("hTh4", [128, NE, 128], BF16, ph)
            hTl = kb.sb("hTl4", [128, NE, 128], BF16, ph)

            prev_use = [None, None]
            prev_ev = [None] * 4
            prev_hq = None
            prev_dT = [None, None]
            prev_hT = [None, None]
            for t in range(NT):
                par = t % 2
                t1 = kb.dma("sync", xt[par][:, :],
                            stream_d[t * 128:(t + 1) * 128, :],
                            waits=[prev_use[par]])
                t_ln, t_lnT = prep_lnT(xt[par][:, :], lnf, lnhi, lnlo,
                                       lnThi, lnTlo, waits=[t1],
                                       pt=(ptA, ptB))
                prev_use[par] = t_ln
                ev = []
                for cc in range(4):
                    cs, ce = cc * 512, (cc + 1) * 512
                    bcs = nh * 2048 + cs
                    n_mm = NE * 3 + 2
                    idx = 0
                    tok = None
                    for e in range(NE):
                        for (a, b) in ((0, 0), (0, 1), (1, 0)):
                            w0 = (list(t_lnT) + [prev_ev[cc]]) if idx == 0 \
                                else ()
                            tok = kb.mm(pb[cc][:, 0:512],
                                        ((lnThi, lnTlo)[a])[:, e, :],
                                        w1[b][:, e, cs:ce],
                                        start=(idx == 0), stop=False,
                                        waits=w0, fd=512)
                            idx += 1
                    kb.mm(pb[cc][:, 0:512], ones1[:, :],
                          b1[0][:, bcs:bcs + 512], False, False, fd=512)
                    tok = kb.mm(pb[cc][:, 0:512], ones1[:, :],
                                b1[1][:, bcs:bcs + 512], False, True, fd=512)
                    tev = kb.act(hsb[:, cs:ce], pb[cc][:, 0:512], AF.Copy,
                                 waits=[tok, prev_hq], fd=512)
                    ev.append(tev)
                t_hq = None
                for ch in range(2):
                    z = hsb[:, ch * 1024:(ch + 1) * 1024]
                    w0 = [ev[2 * ch], ev[2 * ch + 1], prev_hq]
                    t_xq = fix8(kb, S["xq"], z, S, w0)
                    t_c1 = kb.vop("ts", S["b"], S["xq"], -4.0, None, OP.max,
                                  waits=[t_xq], fd=1024)
                    t_c2 = kb.vop("ts", S["u"], S["b"], 4.0, None, OP.min,
                                  waits=[t_c1], fd=1024)
                    t_t = kb.vop("ts", S["tt"], S["u"], 4.0, 16.0,
                                 OP.mult, OP.add, waits=[t_c2], fd=1024)
                    t_u2 = kb.vop("ts", S["u"], S["tt"], 1.0, -0.5,
                                  OP.mult, OP.add, waits=[t_t], fd=1024)
                    t_fl = _rne_core(kb, S["g"], S["u"], S, [t_u2])
                    t_fi = kb.vop("ts", S["fi"], S["g"], 31.0, None, OP.min,
                                  waits=[t_fl], fd=1024)
                    t_xg = kb.vop("ts", S["xg"], S["fi"], 0.25, -4.0,
                                  OP.mult, OP.add, waits=[t_fi], fd=1024)
                    t_fr = kb.vop("tt", S["fr"], S["tt"], S["fi"],
                                  OP.subtract, waits=[t_fi], fd=1024)
                    t_e0 = kb.act(S["e0"], S["xg"], AF.Erf, scale=ISQ2,
                                  waits=[t_xg], fd=1024)
                    t_xh = kb.vop("ts", S["u"], S["xg"], 0.5, None, OP.mult,
                                  waits=[t_xg], fd=1024)
                    t_g0 = kb.vop("stt", S["y0"], S["e0"], 1.0, S["u"],
                                  OP.add, OP.mult, waits=[t_e0, t_xh],
                                  fd=1024)
                    t_x1 = kb.vop("ts", S["b"], S["xg"], 1.0, 0.25,
                                  OP.mult, OP.add, waits=[t_g0], fd=1024)
                    t_e1 = kb.act(S["e0"], S["b"], AF.Erf, scale=ISQ2,
                                  waits=[t_x1], fd=1024)
                    t_x1h = kb.vop("ts", S["u"], S["b"], 0.5, None, OP.mult,
                                   waits=[t_x1], fd=1024)
                    t_g1 = kb.vop("stt", S["y1"], S["e0"], 1.0, S["u"],
                                  OP.add, OP.mult, waits=[t_e1, t_x1h],
                                  fd=1024)
                    t_dy = kb.vop("tt", S["b"], S["y1"], S["y0"],
                                  OP.subtract, waits=[t_g1], fd=1024)
                    t_fd = kb.vop("tt", S["u"], S["fr"], S["b"], OP.mult,
                                  waits=[t_dy, t_fr], fd=1024)
                    t_y = kb.vop("tt", S["y1"], S["u"], S["y0"], OP.add,
                                 waits=[t_fd], fd=1024)
                    t_m2 = kb.vop("ts", S["b"], S["xq"], -4.0, None,
                                  OP.is_ge, waits=[t_y], fd=1024)
                    t_ym = kb.vop("tt", S["y0"], S["y1"], S["b"], OP.mult,
                                  waits=[t_m2], fd=1024)
                    t_m1 = kb.vop("ts", S["b"], S["xq"], 4.0, None,
                                  OP.is_gt, waits=[t_ym], fd=1024)
                    t_d1 = kb.vop("tt", S["u"], S["xq"], S["y0"],
                                  OP.subtract, waits=[t_ym], fd=1024)
                    t_md = kb.vop("tt", S["fr"], S["b"], S["u"], OP.mult,
                                  waits=[t_m1, t_d1], fd=1024)
                    t_yf = kb.vop("tt", S["y1"], S["y0"], S["fr"], OP.add,
                                  waits=[t_md], fd=1024)
                    t_q = fix8(kb, S["xq"], S["y1"], S, [t_yf])
                    t_hq = kb.split(hqh[:, ch * 1024:(ch + 1) * 1024],
                                    hql[:, ch * 1024:(ch + 1) * 1024],
                                    S["xq"], waits=[t_q, prev_hT[0],
                                                    prev_hT[1]], fd=1024)
                prev_hq = t_hq
                prev_ev = ev
                for half, (srcq, dstq, bankq, evb) in enumerate(
                        ((hqh, hTh, ptA, 0), (hql, hTl, ptB, 1))):
                    for rnd in range(2):
                        tpr = []
                        for ii in range(8):
                            ci = rnd * 8 + ii
                            tpr.append(kb.tp(
                                bankq[:, ii * 128:(ii + 1) * 128],
                                srcq[:, ci * 128:(ci + 1) * 128],
                                waits=([t_hq, prev_hT[half]]
                                       if ii == 0 else ())))
                        tev = kb.act(dstq[:, :, :],
                                     bankq[:, :].rearrange(
                                         "p (c n) -> p c n", n=128),
                                     AF.Copy, waits=[tpr[-1], prev_dT[half]],
                                     fd=1024)
                        prev_hT[half] = tev
                        prev_dT[half] = kb.dma(
                            "gpsimd",
                            yqT_r[half][:, nh * 16 + rnd * 8:
                                        nh * 16 + (rnd + 1) * 8,
                                        t * 128:(t + 1) * 128],
                            dstq[:, :, :], waits=[tev])
            P.barrier()

    with contextlib.ExitStack() as ph:
        w2 = load_w(wnames["F1w2"], 4 * E, E, ph)
        b2 = load_bias(wnames["F1b2"], E, ph)
        P.barrier()

        hTh = kb.sb("hTp2h", [128, 32, 128], BF16, ph)
        hTl = kb.sb("hTp2l", [128, 32, 128], BF16, ph)
        ffo = kb.sb("ffo4", [128, 1024], F32, ph)
        S = {n: kb.sb("q_" + n, [128, 1024], F32, ph) for n in
             ("u", "a", "f1", "b", "f2", "g")}
        xres = S["a"]
        xout = ffo
        lnat = S["b"]

        prev_mm = None
        prev_ev = [None, None]
        prev_fix = None
        prev_xo = None
        prev_od = None
        prev_ln = None
        for t in range(NT):
            th1 = kb.dma("sync", hTh[:, :, :],
                         yqT_r[0][:, :, t * 128:(t + 1) * 128],
                         waits=[prev_mm])
            th2 = kb.dma("sync", hTl[:, :, :],
                         yqT_r[1][:, :, t * 128:(t + 1) * 128])
            toks = []
            for half in range(2):
                cs, ce = half * 512, (half + 1) * 512
                n_mm = 32 * 3 + 2
                idx = 0
                tok = None
                for e in range(32):
                    for (a, b) in ((0, 0), (0, 1), (1, 0)):
                        w0 = ([th2, prev_ev[half]]) if idx == 0 else ()
                        tok = kb.mm(pb[half][:, 0:512],
                                    ((hTh, hTl)[a])[:, e, :],
                                    w2[b][:, e, cs:ce],
                                    start=(idx == 0), stop=False,
                                    waits=w0, fd=512)
                        idx += 1
                kb.mm(pb[half][:, 0:512], ones1[:, :], b2[0][:, cs:ce],
                      False, False, fd=512)
                tok = kb.mm(pb[half][:, 0:512], ones1[:, :], b2[1][:, cs:ce],
                            False, True, fd=512)
                toks.append(tok)
            prev_mm = toks[-1]
            te1 = kb.act(ffo[:, 0:512], pb[0][:, 0:512], AF.Copy,
                         waits=[toks[0], prev_fix, prev_od], fd=512)
            te2 = kb.act(ffo[:, 512:1024], pb[1][:, 0:512], AF.Copy,
                         waits=[toks[1]], fd=512)
            prev_ev = [te1, te2]
            t_q = fix8(kb, S["u"], ffo[:, :], S, [te1, te2, prev_ln])
            prev_fix = t_q
            t_ln = kb.ln(lnat, S["u"], sc, waits=[t_q])
            prev_ln = t_ln
            t_xr = kb.dma("sync", xres[:, :],
                          stream_d[t * 128:(t + 1) * 128, :],
                          waits=[prev_xo, t_q])
            t_xo = kb.vop("tt", xout[:, :], xres[:, :], lnat[:, :], OP.add,
                          waits=[t_ln, t_xr, prev_od], fd=1024)
            prev_xo = t_xo
            prev_od = kb.dma("gpsimd", out_d[t * 128:(t + 1) * 128, :],
                             xout[:, :], waits=[t_xo])
    P.barrier()


def _mk_inputs(inputs, c):
    x = np.ascontiguousarray(inputs["L0_attn_query"], dtype=np.float32)
    d = {}
    d["xb"] = np.ascontiguousarray(x[c])
    d["xs"] = np.ascontiguousarray(x[:, c * SEQS:(c + 1) * SEQS, :])
    d["posc"] = np.arange(c * SEQS, (c + 1) * SEQS, dtype=np.float32)
    d["cbt"] = np.tile(np.array([[v for _, v in CONSTS]], np.float32),
                       (128, 1))
    d["identb"] = np.eye(128, dtype=np.float32).astype(ml_dtypes.bfloat16)
    d["iota16"] = np.tile(np.arange(16, dtype=np.float32)[None, :], (128, 1))
    d["L0Wq"] = inputs["L0_attn_Wq"]
    d["L0Wk"] = inputs["L0_attn_Wk"]
    d["L0Wv"] = inputs["L0_attn_Wv"]
    d["L0Wo"] = inputs["L0_attn_Wo"]
    d["L0bq"] = inputs["L0_attn_bq"]
    d["L0bk"] = inputs["L0_attn_bk"]
    d["L0bv"] = inputs["L0_attn_bv"]
    d["L0bo"] = inputs["L0_attn_bo"]
    d["nodecat"] = np.concatenate(
        [np.pad(inputs[f"L0_ffn_{f}_node_w"].T, ((0, 0), (0, 1)))
         for f in ("q", "k", "v")], axis=1).astype(np.float32)
    d["nodeb"] = np.concatenate(
        [np.pad(inputs[f"L0_ffn_{f}_node_b"], (0, 1)) for f in
         ("q", "k", "v")]).astype(np.float32)
    for f in ("q", "k", "v"):
        d[f"W1{f}"] = np.ascontiguousarray(
            np.transpose(inputs[f"L0_ffn_{f}_w1"], (1, 0, 2)).reshape(E, E))
        d[f"W2{f}"] = np.ascontiguousarray(
            inputs[f"L0_ffn_{f}_w2"].reshape(E, E))
        d[f"b1{f}"] = np.ascontiguousarray(
            inputs[f"L0_ffn_{f}_b1"].reshape(E))
        d[f"b2{f}"] = np.ascontiguousarray(inputs[f"L0_ffn_{f}_b2"])
    d["L1WqT"] = np.ascontiguousarray(inputs["L1_attn_Wq"].T)
    d["L1WkT"] = np.ascontiguousarray(inputs["L1_attn_Wk"].T)
    d["L1WvT"] = np.ascontiguousarray(inputs["L1_attn_Wv"].T)
    d["L1WoT"] = np.ascontiguousarray(inputs["L1_attn_Wo"].T)
    d["F1w1"] = inputs["L1_ffn_w1"]
    d["F1b1"] = inputs["L1_ffn_b1"]
    d["F1w2"] = inputs["L1_ffn_w2"]
    d["F1b2"] = inputs["L1_ffn_b2"]
    return {k2: np.ascontiguousarray(v) for k2, v in d.items()}


_NBLOCKS = 4


def kernel(**inputs):
    nc = build_nc(nblocks=_NBLOCKS, use_cc=True)
    in_maps = [_mk_inputs(inputs, c) for c in range(B)]
    res = run_bass_kernel_spmd(nc, in_maps, core_ids=list(range(B)))
    return np.stack([res.results[c]["out"] for c in range(B)], axis=0)
